# version 01
# baseline (speedup 1.0000x reference)
import math
from contextlib import ExitStack

import numpy as np
import ml_dtypes

import concourse.bass as bass
import concourse.mybir as mybir
from concourse.bass_utils import run_bass_kernel_spmd

F32 = mybir.dt.float32
BF16 = mybir.dt.bfloat16
AF = mybir.ActivationFunctionType
ALU = mybir.AluOpType
AX = mybir.AxisListType
BF = ml_dtypes.bfloat16

N_CORES = 8
D = 1024
T_CORE = 2048
HALO = 512
SEQ = 8192
E_MOE = 8
CAP = 2048
NEG = -1.0e9

PE, ACT, DVE, POOL, SP = "pe", "act", "dve", "pool", "sp"
QSP, QPOOL, QACT = "qsp", "qpool", "qact"
COMPUTE_ENGINES = (PE, ACT, DVE, POOL, SP)
DMA_QUEUES = (QSP, QPOOL, QACT)
QUEUE_ISSUER = {QSP: SP, QPOOL: POOL, QACT: ACT}
ENGINE_ATTR = {PE: "tensor", ACT: "scalar", DVE: "vector", POOL: "gpsimd", SP: "sync"}


class Buf:

    __slots__ = ("name", "t", "regions")

    def __init__(self, name, t):
        self.name = name
        self.t = t
        self.regions = {}

    def __getitem__(self, idx):
        return self.t[idx]

    @property
    def ap(self):
        return self.t


def _conflict(k1, k2):
    return k1 == "" or k2 == "" or k1 == k2


class Prog:

    NQ = 6

    def __init__(self, nc, stack):
        self.nc = nc
        self.stack = stack
        self.queues = {e: [] for e in COMPUTE_ENGINES}
        self.cnt = {e: 0 for e in COMPUTE_ENGINES}
        self.drained = {e: 0 for e in COMPUTE_ENGINES}
        self.sems = {e: stack.enter_context(nc.semaphore(f"sem_{e}"))
                     for e in COMPUTE_ENGINES}
        self.qsem = {}
        self.quse = {q: 0 for q in DMA_QUEUES}
        for q in DMA_QUEUES:
            for i in range(self.NQ):
                sid = q + str(i)
                self.qsem[sid] = stack.enter_context(nc.semaphore(f"sem_{sid}"))
                self.cnt[sid] = 0
        all_src = list(COMPUTE_ENGINES) + list(self.qsem.keys())
        self.waited = {e: {s: 0 for s in all_src} for e in COMPUTE_ENGINES}
        self.bufs = []

    def _sem_of(self, src):
        return self.sems[src] if src in self.sems else self.qsem[src]

    def buf(self, t, name=None):
        b = Buf(name or getattr(t, "name", "b%d" % len(self.bufs)), t)
        self.bufs.append(b)
        return b

    def sbuf(self, name, shape, dtype, stack=None):
        t = (stack or self.stack).enter_context(self.nc.sbuf_tensor(name, shape, dtype))
        return self.buf(t, name)

    def psum(self, name, shape, dtype=F32):
        t = self.stack.enter_context(self.nc.psum_tensor(name, shape, dtype))
        return self.buf(t, name)

    def alias(self, orig, ap, name):
        b = Buf(name, ap)
        b.regions = orig.regions
        return b

    def _collect(self, reads, writes):
        raww = []
        warr = []
        for b, k in reads:
            for k2, st in b.regions.items():
                if _conflict(k, k2) and st[0] is not None:
                    raww.append(st[0])
        for b, k in writes:
            for k2, st in b.regions.items():
                if _conflict(k, k2):
                    if st[0] is not None:
                        raww.append(st[0])
                    warr.extend(st[1].items())
        return raww, warr

    def _edges_to_waits(self, eng, raww, warr):
        need_drain = False
        waits = {}
        for src, cnt in raww:
            if src == eng:
                if eng != PE and cnt > self.drained[eng]:
                    need_drain = True
            else:
                if cnt > self.waited[eng][src]:
                    waits[src] = max(waits.get(src, 0), cnt)
        for src, cnt in warr:
            if src == eng:
                if eng != PE and cnt > self.drained[eng]:
                    need_drain = True
                continue
            if cnt > self.waited[eng][src]:
                waits[src] = max(waits.get(src, 0), cnt)
        return need_drain, waits

    def _apply(self, reads, writes, src_cnt):
        src, cnt = src_cnt
        for b, k in reads:
            st = b.regions.setdefault(k, [None, {}])
            st[1][src] = max(st[1].get(src, 0), cnt)
        for b, k in writes:
            st = b.regions.setdefault(k, [None, {}])
            st[0] = src_cnt
            st[1] = {}

    def emit(self, eng, fn, reads=(), writes=(), n_inc=1):
        raww, warr = self._collect(reads, writes)
        need_drain, waits = self._edges_to_waits(eng, raww, warr)
        if need_drain:
            self.queues[eng].append(("drain", None, None))
            self.drained[eng] = self.cnt[eng]
        for src in waits:
            self.waited[eng][src] = waits[src]
        self.cnt[eng] += n_inc
        cnt = self.cnt[eng]
        self.queues[eng].append(
            ("op", dict(waits=waits, fn=fn, sem=self.sems[eng], n_inc=n_inc), None))
        self._apply(reads, writes, (eng, cnt))
        return (eng, cnt)

    def dma(self, queue, fn, reads=(), writes=()):
        eng = QUEUE_ISSUER[queue]
        sid = queue + str(self.quse[queue] % self.NQ)
        self.quse[queue] += 1
        raww, warr = self._collect(reads, writes)
        need_drain, waits = self._edges_to_waits(eng, raww, warr)
        prev = self.cnt[sid]
        if prev > self.waited[eng][sid]:
            waits[sid] = max(waits.get(sid, 0), prev)
        if need_drain:
            self.queues[eng].append(("drain", None, None))
            self.drained[eng] = self.cnt[eng]
        for src in waits:
            self.waited[eng][src] = waits[src]
        self.cnt[sid] += 16
        cnt = self.cnt[sid]
        self.queues[eng].append(
            ("op", dict(waits=waits, fn=fn, sem=self.qsem[sid], n_inc=16), None))
        self._apply(reads, writes, (sid, cnt))
        return (sid, cnt)

    def wait_all_dma(self, eng=SP):
        for sid in self.qsem:
            if self.cnt[sid] > self.waited[eng][sid]:
                tgt = self.cnt[sid]
                self.waited[eng][sid] = tgt
                self.queues[eng].append(
                    ("op", dict(waits={}, fn=None, sem=None, n_inc=0,
                                raw_wait=(self.qsem[sid], tgt)), None))

    def barrier(self):
        targets = {}
        for e in COMPUTE_ENGINES:
            self.queues[e].append(("drain", None, None))
            self.drained[e] = self.cnt[e]
            self.cnt[e] += 1
            targets[e] = self.cnt[e]
            self.queues[e].append(
                ("op", dict(waits={}, fn=lambda E: E.nop(), sem=self.sems[e],
                            n_inc=1), None))
        for e in COMPUTE_ENGINES:
            for s in COMPUTE_ENGINES:
                if s != e and targets[s] > self.waited[e][s]:
                    self.waited[e][s] = targets[s]
                    sem = self.sems[s]
                    self.queues[e].append(
                        ("op", dict(waits={}, fn=None, sem=None, n_inc=0,
                                    raw_wait=(sem, targets[s])), None))
            for sid in self.qsem:
                if self.cnt[sid] > self.waited[e][sid]:
                    self.waited[e][sid] = self.cnt[sid]
                    self.queues[e].append(
                        ("op", dict(waits={}, fn=None, sem=None, n_inc=0,
                                    raw_wait=(self.qsem[sid], self.cnt[sid])), None))
        for b in self.bufs:
            b.regions.clear()

    def finalize(self, block):
        for e in COMPUTE_ENGINES:
            q = self.queues[e]
            if not q:
                continue
            dec = getattr(block, ENGINE_ATTR[e])

            def body(E, q=q, e=e):
                for kind, info, _ in q:
                    if kind == "drain":
                        E.drain()
                        continue
                    if info.get("raw_wait"):
                        sem, tgt = info["raw_wait"]
                        E.wait_ge(sem, tgt)
                        continue
                    for src, tgt in info["waits"].items():
                        E.wait_ge(self._sem_of(src), tgt)
                    inst = info["fn"](E)
                    if info["sem"] is not None:
                        inst.then_inc(info["sem"], info["n_inc"])
            dec(body)



def matmul_acc(p, psum_ap, pairs, psum_buf, reads, n_inc=None, start=True, stop=True):
    def fn(E):
        n = len(pairs)
        inst = None
        for i, (l, r) in enumerate(pairs):
            inst = E.matmul(psum_ap, l, r, start=(start and i == 0),
                            stop=(stop and i == n - 1))
        return inst
    return p.emit(PE, fn, reads=reads, writes=[(psum_buf, "")],
                  n_inc=1 if n_inc is None else n_inc)


def feat_chunk(buf, C, T, j, t0=0, t1=None):
    t1 = T if t1 is None else t1
    return buf[:, j * T + t0: j * T + t1]



def _win_masks():
    kl = np.arange(128)[:, None]
    u = np.arange(1024)[None, :] - 512
    w1 = (u < kl).astype(BF)
    w2 = (u >= kl).astype(BF)
    return np.concatenate([w1, w2], axis=1)


def win_mask_ap(masks, j):
    if j < 4:
        off = 512 - 128 * j
    else:
        off = 1024 + 512 - 128 * (j - 4)
    return masks[:, off: off + 512]


def _tri_excl():
    t = np.arange(128)
    return (t[:, None] < t[None, :]).astype(BF)


def to_feat(a):
    T, C = a.shape
    return np.ascontiguousarray(a.T.reshape(C // 128, 128, T).transpose(1, 0, 2).reshape(128, -1))


def from_feat(a, T, C):
    return np.ascontiguousarray(a.reshape(128, C // 128, T).transpose(1, 0, 2).reshape(C, T).T)



class Ctx:

    def __init__(self, p, consts):
        self.p = p
        self.c = consts


def emit_ln(ctx, x_f32, T, xn_out, pb, scr, rstd_row, x_row=None, x_off=0, xn_row=None, xn_off=0, rs_off=0):
    p = ctx.p
    c = ctx.c
    x_row = x_row or T
    xn_row = xn_row or T
    nt = T // 512
    for tt in range(nt):
        t0 = tt * 512
        xi = lambda j, t0=t0: x_f32[:, j * x_row + x_off + t0: j * x_row + x_off + t0 + 512]
        pairs_m = [(c["ones_f32_128x1"][:, 0:1], xi(j)) for j in range(8)]
        matmul_acc(p, pb[0][0:1, 0:512], pairs_m, pb[0],
                   reads=[(x_f32, ""), (c["ones_f32_128x1"], "")])
        for j in range(8):
            xap = xi(j)
            def fsq(E, xap=xap):
                return E.tensor_tensor(scr["tmp"][:, 0:512], xap, xap, op=ALU.mult)
            p.emit(DVE, fsq, reads=[(x_f32, "")], writes=[(scr["tmp"], "")])
            matmul_acc(p, pb[1][0:1, 0:512],
                       [(c["ones_f32_128x1"][:, 0:1], scr["tmp"][:, 0:512])],
                       pb[1], reads=[(scr["tmp"], ""), (c["ones_f32_128x1"], "")],
                       start=(j == 0), stop=(j == 7))
        row = scr["ra"]
        def f1(E, t0=t0):
            return E.tensor_scalar(row[0:1, 0:512], pb[0][0:1, 0:512],
                                   1.0 / D, None, op0=ALU.mult)
        p.emit(DVE, f1, reads=[(pb[0], "")], writes=[(row, "")])
        def f2(E, t0=t0):
            return E.tensor_scalar(scr["rb"][0:1, 0:512], pb[1][0:1, 0:512],
                                   1.0 / D, 1e-5, op0=ALU.mult, op1=ALU.add)
        p.emit(DVE, f2, reads=[(pb[1], "")], writes=[(scr["rb"], "")])
        def f3b(E, t0=t0):
            return E.tensor_tensor(scr["rc"][0:1, 0:512],
                                   row[0:1, 0:512], row[0:1, 0:512], op=ALU.mult)
        p.emit(DVE, f3b, reads=[(row, "")], writes=[(scr["rc"], "")])
        def f3c(E, t0=t0):
            return E.tensor_tensor(scr["rb"][0:1, 0:512], scr["rb"][0:1, 0:512],
                                   scr["rc"][0:1, 0:512], op=ALU.subtract)
        p.emit(DVE, f3c, reads=[(scr["rb"], ""), (scr["rc"], "")],
               writes=[(scr["rb"], "")])
        def f4(E, t0=t0):
            return E.activation(scr["rc"][0:1, 0:512], scr["rb"][0:1, 0:512],
                                AF.Ln, bias=0.0, scale=1.0)
        p.emit(ACT, f4, reads=[(scr["rb"], "")], writes=[(scr["rc"], "")])
        def f5(E, t0=t0):
            return E.activation(scr["rb"][0:1, 0:512],
                                scr["rc"][0:1, 0:512], AF.Exp,
                                bias=0.0, scale=-0.5)
        p.emit(ACT, f5, reads=[(scr["rc"], "")], writes=[(scr["rb"], "")])
        if rstd_row is not None:
            def f5b(E, t0=t0):
                return E.tensor_copy(rstd_row[0:1, rs_off + t0:rs_off + t0 + 512],
                                     scr["rb"][0:1, 0:512])
            p.emit(DVE, f5b, reads=[(scr["rb"], "")],
                   writes=[(rstd_row, "t%d" % (rs_off + t0))])
        def f6(E, t0=t0):
            return E.tensor_tensor(scr["rc"][0:1, 0:512], row[0:1, 0:512],
                                   scr["rb"][0:1, 0:512], op=ALU.mult)
        p.emit(DVE, f6, reads=[(row, ""), (scr["rb"], "")],
               writes=[(scr["rc"], "")])
        matmul_acc(p, pb[2][:, 0:512],
                   [(c["ones_f32_1x128"][0:1, :], scr["rb"][0:1, 0:512])],
                   pb[2], reads=[(scr["rb"], ""), (c["ones_f32_1x128"], "")])
        matmul_acc(p, pb[3][:, 0:512],
                   [(c["nones_f32_1x128"][0:1, :], scr["rc"][0:1, 0:512])],
                   pb[3], reads=[(scr["rc"], ""), (c["nones_f32_1x128"], "")])
        for j in range(8):
            xap = xi(j)
            def f7(E, xap=xap):
                return E.tensor_tensor(scr["tmp"][:, 0:512], xap,
                                       pb[2][:, 0:512], op=ALU.mult)
            p.emit(DVE, f7, reads=[(x_f32, ""), (pb[2], "")], writes=[(scr["tmp"], "")])
            def f8(E, j=j, t0=t0):
                return E.tensor_tensor(
                    xn_out[:, j * xn_row + xn_off + t0: j * xn_row + xn_off + t0 + 512],
                    scr["tmp"][:, 0:512], pb[3][:, 0:512], op=ALU.add)
            p.emit(DVE, f8, reads=[(scr["tmp"], ""), (pb[3], "")],
                   writes=[(xn_out, "c%d" % j)])




def emit_proj(ctx, w_sb, in_feat, T, Cin, Cout, out_writer, t_tiles=None,
              bias_sb=None, pb_sel=None, rhs_region=""):
    p = ctx.p
    nji = Cin // 128
    njo = Cout // 128
    tt_list = t_tiles if t_tiles is not None else [(t0, min(t0 + 512, T)) for t0 in range(0, T, 512)]
    k = 0
    for (t0, t1) in tt_list:
        for jo in range(njo):
            pb = pb_sel[k % len(pb_sel)]
            k += 1
            pairs = [(w_sb[:, ji * Cout + jo * 128: ji * Cout + jo * 128 + 128],
                      in_feat[:, ji * T + t0: ji * T + t1]) for ji in range(nji)]
            matmul_acc(p, pb[0:128, 0:t1 - t0], pairs, pb,
                       reads=[(w_sb, ""), (in_feat, rhs_region)])
            bias_ap = bias_sb[:, jo:jo + 1] if bias_sb is not None else None
            out_writer(jo, t0, t1, pb, bias_ap)


def emit_absmax_norm_bounds(ctx, feat, T, nheads, out_row, scr, pbx,
                            chunk_heads=2):
    p = ctx.p
    c = ctx.c
    nj = (64 * nheads) // 128
    nt = T // 512
    for tt in range(nt):
        t0 = tt * 512
        pairs = []
        for j in range(nj):
            def fsq(E, j=j, t0=t0):
                return E.tensor_tensor(scr["sq"][:, 0:512],
                                       feat[:, j * T + t0: j * T + t0 + 512],
                                       feat[:, j * T + t0: j * T + t0 + 512],
                                       op=ALU.mult)
            p.emit(DVE, fsq, reads=[(feat, "")], writes=[(scr["sq"], "")])
            matmul_acc(p, pbx[0][0:nheads, 0:512],
                       [(c["esel"][:, j * 16: j * 16 + nheads], scr["sq"][:, 0:512])],
                       pbx[0], reads=[(scr["sq"], ""), (c["esel"], "")],
                       start=(j == 0), stop=(j == nj - 1))
        def fmx(E, tt=tt):
            return E.tensor_reduce(scr["nmax_t"][0:nheads, 0:1],
                                   pbx[0][0:nheads, 0:512], axis=AX.X, op=ALU.max)
        p.emit(DVE, fmx, reads=[(pbx[0], "")], writes=[(scr["nmax_t"], "")])
        if tt == 0:
            def fcp(E):
                return E.tensor_copy(scr["nmax"][0:nheads, 0:1],
                                     scr["nmax_t"][0:nheads, 0:1])
            p.emit(DVE, fcp, reads=[(scr["nmax_t"], "")], writes=[(scr["nmax"], "")])
        else:
            def fmg(E):
                return E.tensor_tensor(scr["nmax"][0:nheads, 0:1],
                                       scr["nmax"][0:nheads, 0:1],
                                       scr["nmax_t"][0:nheads, 0:1], op=ALU.max)
            p.emit(DVE, fmg, reads=[(scr["nmax_t"], ""), (scr["nmax"], "")],
                   writes=[(scr["nmax"], "")])
    def fc(E):
        return E.tensor_copy(scr["nmax_bf"][0:nheads, 0:1], scr["nmax"][0:nheads, 0:1])
    p.emit(DVE, fc, reads=[(scr["nmax"], "")], writes=[(scr["nmax_bf"], "")])
    def ftr(E):
        return E.transpose(pbx[1].t.bitcast(BF16)[0:1, 0:nheads],
                           scr["nmax_bf"][0:nheads, 0:1], c["ident"][:, :])
    p.emit(PE, ftr, reads=[(scr["nmax_bf"], ""), (c["ident"], "")], writes=[(pbx[1], "")])
    def fev(E):
        return E.tensor_copy(out_row[0:1, 0:nheads], pbx[1].t.bitcast(BF16)[0:1, 0:nheads])
    p.emit(DVE, fev, reads=[(pbx[1], "")], writes=[(out_row, "")])


def emit_bias_col(ctx, bq_row, bk_row, h_q, h_k, scale, bias_col, pbx, scr):
    p = ctx.p
    c = ctx.c
    def f1(E):
        return E.tensor_tensor(scr["b1"][0:1, 0:1], bq_row[0:1, h_q:h_q + 1],
                               bk_row[0:1, h_k:h_k + 1], op=ALU.mult)
    p.emit(DVE, f1, reads=[(bq_row, ""), (bk_row, "")], writes=[(scr["b1"], "")])
    def f2(E):
        return E.activation(scr["b2"][0:1, 0:1], scr["b1"][0:1, 0:1], AF.Ln,
                            bias=0.0, scale=1.0)
    p.emit(ACT, f2, reads=[(scr["b1"], "")], writes=[(scr["b2"], "")])
    def f3(E):
        return E.activation(scr["b1_bf"][0:1, 0:1], scr["b2"][0:1, 0:1], AF.Exp,
                            bias=0.0, scale=0.5)
    p.emit(ACT, f3, reads=[(scr["b2"], "")], writes=[(scr["b1_bf"], "")])
    matmul_acc(p, pbx[0][0:128, 0:1],
               [(c["ones_bf_1x128"][0:1, :], scr["b1_bf"][0:1, 0:1])],
               pbx[0], reads=[(scr["b1_bf"], ""), (c["ones_bf_1x128"], "")])
    def f4(E):
        return E.tensor_scalar(bias_col[0:128, 0:1], pbx[0][0:128, 0:1],
                               -float(scale), None, op0=ALU.mult)
    p.emit(DVE, f4, reads=[(pbx[0], "")], writes=[(bias_col, "")])


def emit_attn_unit(ctx, q_ap, q_reads, chunks, bias_col, exp_scale, QW,
                   ps_s, ps_av, ps_aux, attT, scr, out_fn):
    p = ctx.p
    c = ctx.c
    n = len(chunks)
    for i, ch in enumerate(chunks):
        sb = ps_s[i % 2]
        at = attT[i % 2]
        matmul_acc(p, sb[0:128, 0:QW], [(ch["k_ap"], q_ap)], sb,
                   reads=list(ch["k_reads"]) + list(q_reads))
        def fexp(E, sb=sb, at=at):
            return E.activation(at[0:128, 0:QW], sb[0:128, 0:QW], AF.Exp,
                                bias=bias_col[0:128, 0:1], scale=float(exp_scale))
        p.emit(ACT, fexp, reads=[(sb, ""), (bias_col, "")], writes=[(at, "")])
        if ch.get("mask") is not None:
            m_ap, m_reads = ch["mask"]
            def fmask(E, at=at, m_ap=m_ap):
                return E.tensor_tensor(at[0:128, 0:QW], at[0:128, 0:QW], m_ap,
                                       op=ALU.mult)
            p.emit(DVE, fmask, reads=[(at, "")] + list(m_reads), writes=[(at, "")])
        if ch.get("vcol") is not None:
            v_ap, v_reads = ch["vcol"]
            def fv(E, at=at, v_ap=v_ap):
                return E.tensor_scalar(at[0:128, 0:QW], at[0:128, 0:QW], v_ap,
                                       None, op0=ALU.mult)
            p.emit(DVE, fv, reads=[(at, "")] + list(v_reads), writes=[(at, "")])
        matmul_acc(p, ps_av[0:65, 0:QW], [(ch["vt_ap"], at[0:128, 0:QW])], ps_av,
                   reads=[(at, "")] + list(ch["vt_reads"]),
                   start=(i == 0), stop=(i == n - 1))
    def frl(E):
        return E.activation(scr["rln"][0:1, 0:QW], ps_av[64:65, 0:QW],
                            AF.Ln, bias=0.0, scale=1.0)
    p.emit(ACT, frl, reads=[(ps_av, "")], writes=[(scr["rln"], "")])
    def frc(E):
        return E.activation(scr["recip"][0:1, 0:QW], scr["rln"][0:1, 0:QW],
                            AF.Exp, bias=0.0, scale=-1.0)
    p.emit(ACT, frc, reads=[(scr["rln"], "")], writes=[(scr["recip"], "")])
    matmul_acc(p, ps_aux[0:64, 0:QW],
               [(c["ones_f32_1x64"][0:1, :], scr["recip"][0:1, 0:QW])],
               ps_aux, reads=[(scr["recip"], ""), (c["ones_f32_1x64"], "")])
    def fev(E):
        return E.tensor_copy(scr["nbc"][0:64, 0:QW], ps_aux[0:64, 0:QW])
    p.emit(DVE, fev, reads=[(ps_aux, "")], writes=[(scr["nbc"], "")])
    out_fn(ps_av, scr["nbc"])


def emit_norm_scalar(ctx, feat_ap, feat_reads, W, nrm_row, idx, scr, pbn):
    p = ctx.p
    c = ctx.c
    nt = (W + 511) // 512
    for tt in range(nt):
        t0, t1 = tt * 512, min(W, tt * 512 + 512)
        def fsq(E, t0=t0, t1=t1):
            return E.tensor_tensor(scr["sq"][0:64, 0:t1 - t0],
                                   feat_ap[0:64, t0:t1], feat_ap[0:64, t0:t1],
                                   op=ALU.mult)
        p.emit(DVE, fsq, reads=list(feat_reads), writes=[(scr["sq"], "")])
        matmul_acc(p, pbn[0:1, 0:t1 - t0],
                   [(c["ones_bf_64x1"][0:64, 0:1], scr["sq"][0:64, 0:t1 - t0])],
                   pbn, reads=[(scr["sq"], ""), (c["ones_bf_64x1"], "")])
        def fmx(E, t0=t0, t1=t1):
            return E.tensor_reduce(scr["b1"][0:1, 0:1], pbn[0:1, 0:t1 - t0],
                                   axis=AX.X, op=ALU.max)
        p.emit(DVE, fmx, reads=[(pbn, "")], writes=[(scr["b1"], "")])
        if tt == 0:
            def fcp(E):
                return E.tensor_copy(nrm_row[0:1, idx:idx + 1], scr["b1"][0:1, 0:1])
            p.emit(DVE, fcp, reads=[(scr["b1"], "")], writes=[(nrm_row, "i%d" % idx)])
        else:
            def fmg(E):
                return E.tensor_tensor(nrm_row[0:1, idx:idx + 1],
                                       nrm_row[0:1, idx:idx + 1],
                                       scr["b1"][0:1, 0:1], op=ALU.max)
            p.emit(DVE, fmg, reads=[(scr["b1"], ""), (nrm_row, "i%d" % idx)],
                   writes=[(nrm_row, "i%d" % idx)])


def emit_l0_attn(ctx, xn_ext, x_f32, W, weights, masks, vcol0, bufs):
    p = ctx.p
    c = ctx.c
    Wd = weights
    Text = 2560
    T = 2048
    x_row = bufs.get("x_row", T)
    x_off = bufs.get("x_off", 0)
    scr = bufs["scr"]
    pb = bufs["pb"]
    for g in range(4):
        kg, vtg = bufs["kg"][g], bufs["vtg"][g]
        def fms(E, vtg=vtg):
            return E.memset(vtg[:, :], 1.0)
        p.emit(DVE, fms, writes=[(vtg, "")])
        for tt in range(Text // 512):
            t0 = tt * 512
            for which, wsb, bcol in (("k", Wd["wk"], 0), ("v", Wd["wv"], 1)):
                pbk = pb[0] if which == "k" else pb[1]
                pairs = [(wsb[:, ji * 256 + g * 64: ji * 256 + g * 64 + 64],
                          xn_ext[:, ji * Text + t0: ji * Text + t0 + 512])
                         for ji in range(8)]
                matmul_acc(p, pbk[0:64, 0:512], pairs, pbk,
                           reads=[(wsb, ""), (xn_ext, "")])
                bias_ap = Wd["bkv"][(g % 2) * 64:(g % 2) * 64 + 64,
                                    bcol * 2 + g // 2: bcol * 2 + g // 2 + 1]
                if which == "k":
                    def fev(E, kg=kg, t0=t0, pbk=pbk, bias_ap=bias_ap):
                        return E.activation(kg[0:64, t0:t0 + 512], pbk[0:64, 0:512],
                                            AF.Identity, bias=bias_ap, scale=1.0)
                    p.emit(ACT, fev, reads=[(pbk, ""), (Wd["bkv"], "")],
                           writes=[(kg, "t%d" % tt)])
                else:
                    def fev2(E, t0=t0, pbk=pbk, bias_ap=bias_ap):
                        return E.activation(scr["vtmp_bf"][0:64, 0:512], pbk[0:64, 0:512],
                                            AF.Identity, bias=bias_ap, scale=1.0)
                    p.emit(ACT, fev2, reads=[(pbk, ""), (Wd["bkv"], "")],
                           writes=[(scr["vtmp_bf"], "")])
                    for cc in range(4):
                        ci = tt * 4 + cc
                        def ftr(E, cc=cc):
                            return E.transpose(bufs["pbt"][0:128, 0:64],
                                               scr["vtmp_bf"][0:64, cc * 128:(cc + 1) * 128],
                                               c["ident"][0:64, 0:64])
                        p.emit(PE, ftr, reads=[(scr["vtmp_bf"], ""), (c["ident"], "")],
                               writes=[(bufs["pbt"], "")])
                        def fcv(E, ci=ci, vtg=vtg):
                            return E.tensor_copy(vtg[:, ci * 65: ci * 65 + 64],
                                                 bufs["pbt"][0:128, 0:64])
                        p.emit(DVE, fcv, reads=[(bufs["pbt"], "")],
                               writes=[(vtg, "c%d" % ci)])
        emit_norm_scalar(ctx, kg.t, [(kg, "")], Text, bufs["nrm"], 32 + g, scr, pb[2])
    wq_loader = Wd.get("wq_loader")
    cur_half = [-1]
    for qt in range(4):
        t0 = qt * 512
        for g in range(4):
            kg, vtg = bufs["kg"][g], bufs["vtg"][g]
            if wq_loader is not None and (g // 2) != cur_half[0]:
                wq_loader(g // 2)
                cur_half[0] = g // 2
            for hh in range(4):
                qh = 4 * g + hh
                qhh = qh - (g // 2) * 8 if wq_loader is not None else qh
                wq_row = 512 if wq_loader is not None else 1024
                pairs = [(Wd["wq"][:, ji * wq_row + qhh * 64: ji * wq_row + qhh * 64 + 64],
                          xn_ext[:, ji * Text + 512 + t0: ji * Text + 512 + t0 + 512])
                         for ji in range(8)]
                matmul_acc(p, pb[3][0:64, 0:512], pairs, pb[3],
                           reads=[(Wd["wq"], ""), (xn_ext, "")])
                bias_ap = Wd["bq"][(qh % 2) * 64:(qh % 2) * 64 + 64,
                                   qh // 2: qh // 2 + 1]
                def fq(E, bias_ap=bias_ap):
                    return E.activation(bufs["q"][0:64, 0:512], pb[3][0:64, 0:512],
                                        AF.Identity, bias=bias_ap, scale=1.0)
                p.emit(ACT, fq, reads=[(pb[3], ""), (Wd["bq"], "")],
                       writes=[(bufs["q"], "")])
                emit_norm_scalar(ctx, bufs["q"].t, [(bufs["q"], "")], 512,
                                 bufs["nrm"], qh, scr, pb[2])
                emit_bias_col(ctx, bufs["nrm"], bufs["nrm"], qh, 32 + g,
                              0.125, scr["bias_col"], [pb[2]], scr)
                chunks = []
                for j in range(8):
                    ci = qt * 4 + j
                    ch = dict(
                        k_ap=kg[0:64, ci * 128:(ci + 1) * 128],
                        k_reads=[(kg, "")],
                        vt_ap=vtg[:, ci * 65: ci * 65 + 65],
                        vt_reads=[(vtg, "")],
                        mask=(win_mask_ap(masks, j), [(masks, "")]),
                    )
                    if qt == 0 and j < 4:
                        ch["vcol"] = (vcol0[0:128, j:j + 1], [(vcol0, "")])
                    chunks.append(ch)

                def out_fn(ps_av, nbc, qh=qh, qt=qt):
                    jo, pb_ = qh // 2, (qh % 2) * 64
                    def fo(E):
                        return E.tensor_tensor(
                            bufs["out_cat"][pb_:pb_ + 64, jo * 512:(jo + 1) * 512],
                            ps_av[0:64, 0:512], nbc[0:64, 0:512], op=ALU.mult)
                    p.emit(DVE, fo, reads=[(ps_av, ""), (nbc, "")],
                           writes=[(bufs["out_cat"], "h%d" % qh)])
                emit_attn_unit(ctx, bufs["q"][0:64, 0:512], [(bufs["q"], "")],
                               chunks, scr["bias_col"], 0.125, 512,
                               [pb[4], pb[5]], pb[6], pb[2], bufs["attT"],
                               scr, out_fn)
        for jo in range(8):
            pairs = [(Wd["wo"][:, ji * 1024 + jo * 128: ji * 1024 + jo * 128 + 128],
                      bufs["out_cat"][:, ji * 512:(ji + 1) * 512]) for ji in range(8)]
            matmul_acc(p, pb[0][0:128, 0:512], pairs, pb[0],
                       reads=[(Wd["wo"], ""), (bufs["out_cat"], "")])
            def fadd(E, jo=jo, t0=t0):
                a = x_f32[:, jo * x_row + x_off + t0: jo * x_row + x_off + t0 + 512]
                return E.tensor_tensor(a, a, pb[0][0:128, 0:512], op=ALU.add)
            p.emit(DVE, fadd, reads=[(pb[0], ""), (x_f32, "")],
                   writes=[(x_f32, "c%d" % jo)])
    for jo in range(8):
        for tt in range(4):
            t0 = tt * 512
            def fb(E, jo=jo, t0=t0):
                a = x_f32[:, jo * x_row + x_off + t0: jo * x_row + x_off + t0 + 512]
                return E.tensor_scalar(a, a, Wd["bo"][:, jo:jo + 1], None, op0=ALU.add)
            p.emit(DVE, fb, reads=[(x_f32, ""), (Wd["bo"], "")],
                   writes=[(x_f32, "c%d" % jo)])



def host_consts():
    import numpy as _np
    col = _np.ones((128, 1), dtype=_np.float32)
    row = _np.zeros((1, 384), dtype=_np.float32)
    row[0, 0:128] = 1.0
    row[0, 128:256] = -1.0
    row[0, 256:384] = 1.0
    bfrow = _np.ones((1, 128), dtype=BF)
    bfcol = _np.ones((128, 1), dtype=BF)
    ident = _np.eye(128, dtype=BF)
    ident_f32 = _np.eye(128, dtype=_np.float32)
    esel = _np.zeros((128, 128), dtype=BF)
    for j in range(8):
        for pp in range(128):
            esel[pp, 16 * j + 2 * j + pp // 64] = 1.0
    return dict(cst_col=col, cst_row=row, cst_bfrow=bfrow, cst_bfcol=bfcol,
                cst_ident=ident, cst_esel=esel, cst_ident_f32=ident_f32)


def setup_consts(p, nc):
    d_col = nc.dram_tensor("cst_col", [128, 1], F32, kind="ExternalInput").ap()
    d_row = nc.dram_tensor("cst_row", [1, 384], F32, kind="ExternalInput").ap()
    d_bfrow = nc.dram_tensor("cst_bfrow", [1, 128], BF16, kind="ExternalInput").ap()
    d_bfcol = nc.dram_tensor("cst_bfcol", [128, 1], BF16, kind="ExternalInput").ap()
    d_ident = nc.dram_tensor("cst_ident", [128, 128], BF16, kind="ExternalInput").ap()
    d_esel = nc.dram_tensor("cst_esel", [128, 128], BF16, kind="ExternalInput").ap()
    d_if32 = nc.dram_tensor("cst_ident_f32", [128, 128], F32, kind="ExternalInput").ap()
    s_col = p.sbuf("c_col", [128, 1], F32)
    s_row = p.sbuf("c_row", [1, 384], F32)
    s_bfrow = p.sbuf("c_bfrow", [1, 128], BF16)
    s_bfcol = p.sbuf("c_bfcol", [128, 1], BF16)
    s_ident = p.sbuf("c_ident", [128, 128], BF16)
    s_esel = p.sbuf("c_esel", [128, 128], BF16)
    s_if32 = p.sbuf("c_if32", [128, 128], F32)
    for sb, dr in ((s_col, d_col), (s_row, d_row), (s_bfrow, d_bfrow),
                   (s_bfcol, d_bfcol), (s_ident, d_ident), (s_esel, d_esel),
                   (s_if32, d_if32)):
        p.dma(QSP, lambda E, sb=sb, dr=dr: E.dma_start(out=sb[:, :], in_=dr[:, :]),
              writes=[(sb, "")])
    d = {
        "ones_f32_128x1": s_col,
        "ones_f32_1x128": p.alias(s_row, s_row.t[0:1, 0:128], "c_r1"),
        "nones_f32_1x128": p.alias(s_row, s_row.t[0:1, 128:256], "c_r2"),
        "ones_f32_1x64": p.alias(s_row, s_row.t[0:1, 0:64], "c_r3"),
        "one_f32_1x1": p.alias(s_row, s_row.t[0:1, 256:257], "c_r4"),
        "ones_bf_1x128": s_bfrow,
        "ones_bf_128x1": s_bfcol,
        "ones_bf_64x1": p.alias(s_bfcol, s_bfcol.t[0:64, 0:1], "c_c1"),
        "ident": s_ident,
        "ident_f32": s_if32,
        "esel": s_esel,
        "_srow": s_row, "_scol": s_col,
    }
    return d


def load_weight(p, nc, dram_name, dram_ap, Cin, Cout, stack=None):
    sb = p.sbuf("w_" + dram_name, [128, (Cin // 128) * Cout], BF16, stack=stack)
    src = dram_ap.rearrange("(i p) o -> p i o", p=128)
    dst = sb.t[:, :].rearrange("p (i o) -> p i o", o=Cout)
    p.dma(QPOOL, lambda E: E.dma_start(out=dst, in_=src), writes=[(sb, "")])
    return sb


def load_feat_input(p, nc, name, shape, dtype, sb_dtype=None, queue=QSP):
    d = nc.dram_tensor(name, shape, dtype, kind="ExternalInput").ap()
    sb = p.sbuf("s_" + name, shape, sb_dtype or dtype)
    p.dma(queue, lambda E: E.dma_start(out=sb[:, :], in_=d[:, :]), writes=[(sb, "")])
    return sb, d


def emit_wg_prep(ctx, wg_f32, bufs):
    p, c = ctx.p, ctx.c
    pairs = [(wg_f32[:, j * 8: j * 8 + 8], c["ones_f32_128x1"][:, 0:1])
             for j in range(8)]
    matmul_acc(p, bufs["pb_r0"][0:8, 0:1], pairs, bufs["pb_r0"],
               reads=[(wg_f32, ""), (c["ones_f32_128x1"], "")])
    def f(E):
        return E.tensor_scalar(bufs["negcc"][0:8, 0:1], bufs["pb_r0"][0:8, 0:1],
                               -1.0 / D, None, op0=ALU.mult)
    p.emit(DVE, f, reads=[(bufs["pb_r0"], "")], writes=[(bufs["negcc"], "")])


def emit_zc_tile(ctx, x_f32, x_row, x_off, wg_f32, bufs):
    p, c = ctx.p, ctx.c
    pairs_z = [(wg_f32[:, j * 8: j * 8 + 8],
                x_f32[:, j * x_row + x_off: j * x_row + x_off + 128])
               for j in range(8)]
    matmul_acc(p, bufs["pb_r0"][0:8, 0:128], pairs_z, bufs["pb_r0"],
               reads=[(wg_f32, ""), (x_f32, "")])
    pairs_m = [(c["ones_f32_128x1"][:, 0:1],
                x_f32[:, j * x_row + x_off: j * x_row + x_off + 128])
               for j in range(8)]
    matmul_acc(p, bufs["pb_r1"][0:1, 0:128], pairs_m, bufs["pb_r1"],
               reads=[(x_f32, ""), (c["ones_f32_128x1"], "")])
    def f1(E):
        return E.tensor_copy(bufs["mrow"][0:1, 0:128], bufs["pb_r1"][0:1, 0:128])
    p.emit(DVE, f1, reads=[(bufs["pb_r1"], "")], writes=[(bufs["mrow"], "")])
    def f2(E):
        return E.tensor_copy(bufs["zrow"][0:8, 0:128], bufs["pb_r0"][0:8, 0:128])
    p.emit(DVE, f2, reads=[(bufs["pb_r0"], "")], writes=[(bufs["zrow"], "")])
    matmul_acc(p, bufs["pb_r1"][0:8, 0:128],
               [(c["ones_f32_1x128"][0:1, 0:8], bufs["mrow"][0:1, 0:128])],
               bufs["pb_r1"], reads=[(bufs["mrow"], ""), (c["ones_f32_1x128"], "")])
    def f3(E):
        return E.scalar_tensor_tensor(bufs["zc"][0:8, 0:128],
                                      bufs["pb_r1"][0:8, 0:128],
                                      bufs["negcc"][0:8, 0:1],
                                      bufs["zrow"][0:8, 0:128],
                                      op0=ALU.mult, op1=ALU.add)
    p.emit(DVE, f3, reads=[(bufs["pb_r1"], ""), (bufs["negcc"], ""),
                           (bufs["zrow"], "")], writes=[(bufs["zc"], "")])


def emit_first_tile(ctx, bufs):
    p, c = ctx.p, ctx.c
    def ftr(E):
        return E.transpose(bufs["pb_rt"][0:128, 0:8], bufs["zc"][0:8, 0:128],
                           c["ident_f32"][0:8, 0:8])
    p.emit(PE, ftr, reads=[(bufs["zc"], ""), (c["ident_f32"], "")],
           writes=[(bufs["pb_rt"], "")])
    def f0(E):
        return E.tensor_copy(bufs["zt"][0:128, 0:8], bufs["pb_rt"][0:128, 0:8])
    p.emit(DVE, f0, reads=[(bufs["pb_rt"], "")], writes=[(bufs["zt"], "")])
    def f1(E):
        return E.tensor_reduce(bufs["mx"][0:128, 0:1], bufs["zt"][0:128, 0:8],
                               axis=AX.X, op=ALU.max)
    p.emit(DVE, f1, reads=[(bufs["zt"], "")], writes=[(bufs["mx"], "")])
    def f2(E):
        return E.tensor_scalar(bufs["mm"][0:128, 0:8], bufs["zt"][0:128, 0:8],
                               bufs["mx"][0:128, 0:1], None, op0=ALU.is_ge)
    p.emit(DVE, f2, reads=[(bufs["zt"], ""), (bufs["mx"], "")],
           writes=[(bufs["mm"], "")])
    def f3(E):
        return E.tensor_tensor_scan(bufs["incl"][0:128, 0:8], bufs["ones8"][0:128, 0:8],
                                    bufs["mm"][0:128, 0:8], 0.0,
                                    op0=ALU.mult, op1=ALU.add)
    p.emit(DVE, f3, reads=[(bufs["mm"], ""), (bufs["ones8"], "")],
           writes=[(bufs["incl"], "")])
    def f4(E):
        return E.tensor_scalar(bufs["incl"][0:128, 0:8], bufs["incl"][0:128, 0:8],
                               1.0, None, op0=ALU.is_le)
    p.emit(DVE, f4, reads=[(bufs["incl"], "")], writes=[(bufs["incl"], "")])
    def f5(E):
        return E.tensor_tensor(bufs["first"][0:128, 0:8], bufs["mm"][0:128, 0:8],
                               bufs["incl"][0:128, 0:8], op=ALU.mult)
    p.emit(DVE, f5, reads=[(bufs["mm"], ""), (bufs["incl"], "")],
           writes=[(bufs["first"], "")])
    def f6(E):
        return E.tensor_copy(bufs["first_bf"][0:128, 0:8], bufs["first"][0:128, 0:8])
    p.emit(DVE, f6, reads=[(bufs["first"], "")], writes=[(bufs["first_bf"], "")])


def emit_routing_own(ctx, x_f32, x_row, x_off, T, rstd_row, wg_f32, base_col,
                     bufs, coef, coef_bf, tile_loader=None):
    p, c = ctx.p, ctx.c
    def fcp(E):
        return E.tensor_copy(bufs["carry"][0:8, 0:1], base_col[0:8, 0:1])
    p.emit(DVE, fcp, reads=[(base_col, "")], writes=[(bufs["carry"], "")])
    for i in range(T // 128):
        if tile_loader is not None:
            xb, xr, xo = tile_loader(i)
            emit_zc_tile(ctx, xb, xr, xo, wg_f32, bufs)
        else:
            emit_zc_tile(ctx, x_f32, x_row, x_off + i * 128, wg_f32, bufs)
        emit_first_tile(ctx, bufs)
        def ft(E, i=i):
            return E.transpose(bufs["pb_rtb"][0:128, 0:1],
                               rstd_row[0:1, i * 128:(i + 1) * 128],
                               c["ident"][0:1, 0:1])
        p.emit(PE, ft, reads=[(rstd_row, ""), (c["ident"], "")],
               writes=[(bufs["pb_rtb"], "")])
        def f0(E):
            return E.tensor_copy(bufs["rstd_col"][0:128, 0:1],
                                 bufs["pb_rtb"][0:128, 0:1])
        p.emit(DVE, f0, reads=[(bufs["pb_rtb"], "")], writes=[(bufs["rstd_col"], "")])
        def f1(E):
            return E.tensor_scalar(bufs["lg"][0:128, 0:8], bufs["zt"][0:128, 0:8],
                                   bufs["rstd_col"][0:128, 0:1], None, op0=ALU.mult)
        p.emit(DVE, f1, reads=[(bufs["zt"], ""), (bufs["rstd_col"], "")],
               writes=[(bufs["lg"], "")])
        def f2(E):
            return E.tensor_reduce(bufs["mx"][0:128, 0:1], bufs["lg"][0:128, 0:8],
                                   axis=AX.X, op=ALU.max)
        p.emit(DVE, f2, reads=[(bufs["lg"], "")], writes=[(bufs["mx"], "")])
        def f3(E):
            return E.tensor_scalar(bufs["nmx"][0:128, 0:1], bufs["mx"][0:128, 0:1],
                                   -1.0, None, op0=ALU.mult)
        p.emit(DVE, f3, reads=[(bufs["mx"], "")], writes=[(bufs["nmx"], "")])
        def f4(E):
            return E.activation(bufs["ex"][0:128, 0:8], bufs["lg"][0:128, 0:8],
                                AF.Exp, bias=bufs["nmx"][0:128, 0:1], scale=1.0)
        p.emit(ACT, f4, reads=[(bufs["lg"], ""), (bufs["nmx"], "")],
               writes=[(bufs["ex"], "")])
        def f5(E):
            return E.tensor_reduce(bufs["sm"][0:128, 0:1], bufs["ex"][0:128, 0:8],
                                   axis=AX.X, op=ALU.add)
        p.emit(DVE, f5, reads=[(bufs["ex"], "")], writes=[(bufs["sm"], "")])
        def f6(E):
            return E.tensor_tensor(bufs["fe"][0:128, 0:8], bufs["first"][0:128, 0:8],
                                   bufs["ex"][0:128, 0:8], op=ALU.mult)
        p.emit(DVE, f6, reads=[(bufs["first"], ""), (bufs["ex"], "")],
               writes=[(bufs["fe"], "")])
        def f7(E):
            return E.tensor_reduce(bufs["gs"][0:128, 0:1], bufs["fe"][0:128, 0:8],
                                   axis=AX.X, op=ALU.add)
        p.emit(DVE, f7, reads=[(bufs["fe"], "")], writes=[(bufs["gs"], "")])
        def f8(E):
            return E.reciprocal(bufs["rcp"][0:128, 0:1], bufs["sm"][0:128, 0:1])
        p.emit(DVE, f8, reads=[(bufs["sm"], "")], writes=[(bufs["rcp"], "")])
        def f9(E):
            return E.tensor_tensor(bufs["gate"][0:128, 0:1], bufs["gs"][0:128, 0:1],
                                   bufs["rcp"][0:128, 0:1], op=ALU.mult)
        p.emit(DVE, f9, reads=[(bufs["gs"], ""), (bufs["rcp"], "")],
               writes=[(bufs["gate"], "")])
        def f10(E):
            return E.tensor_scalar(bufs["fg_bf"][0:128, 0:8], bufs["first"][0:128, 0:8],
                                   bufs["gate"][0:128, 0:1], None, op0=ALU.mult)
        p.emit(DVE, f10, reads=[(bufs["first"], ""), (bufs["gate"], "")],
               writes=[(bufs["fg_bf"], "")])
        def ft1(E):
            return E.transpose(bufs["pb_rtb"][0:8, 0:128], bufs["first_bf"][0:128, 0:8],
                               c["ident"][0:128, 0:128])
        p.emit(PE, ft1, reads=[(bufs["first_bf"], ""), (c["ident"], "")],
               writes=[(bufs["pb_rtb"], "")])
        def fe1(E):
            return E.tensor_copy(bufs["fe_e"][0:8, 0:128], bufs["pb_rtb"][0:8, 0:128])
        p.emit(DVE, fe1, reads=[(bufs["pb_rtb"], "")], writes=[(bufs["fe_e"], "")])
        def ft2(E):
            return E.transpose(bufs["pb_rtb"][0:8, 0:128], bufs["fg_bf"][0:128, 0:8],
                               c["ident"][0:128, 0:128])
        p.emit(PE, ft2, reads=[(bufs["fg_bf"], ""), (c["ident"], "")],
               writes=[(bufs["pb_rtb"], "")])
        def fe2(E):
            return E.tensor_copy(bufs["fg_e"][0:8, 0:128], bufs["pb_rtb"][0:8, 0:128])
        p.emit(DVE, fe2, reads=[(bufs["pb_rtb"], "")], writes=[(bufs["fg_e"], "")])
        def fsc(E):
            return E.tensor_tensor_scan(bufs["incl_e"][0:8, 0:128],
                                        bufs["ones8"][0:8, 0:128],
                                        bufs["fe_e"][0:8, 0:128],
                                        bufs["carry"][0:8, 0:1],
                                        op0=ALU.mult, op1=ALU.add)
        p.emit(DVE, fsc, reads=[(bufs["fe_e"], ""), (bufs["ones8"], ""),
                                (bufs["carry"], "")], writes=[(bufs["incl_e"], "")])
        def fcar(E):
            return E.tensor_copy(bufs["carry"][0:8, 0:1], bufs["incl_e"][0:8, 127:128])
        p.emit(DVE, fcar, reads=[(bufs["incl_e"], "")], writes=[(bufs["carry"], "")])
        def fex(E):
            return E.tensor_tensor(bufs["excl"][0:8, 0:128], bufs["incl_e"][0:8, 0:128],
                                   bufs["fe_e"][0:8, 0:128], op=ALU.subtract)
        p.emit(DVE, fex, reads=[(bufs["incl_e"], ""), (bufs["fe_e"], "")],
               writes=[(bufs["excl"], "")])
        def fk(E):
            return E.tensor_scalar(bufs["excl"][0:8, 0:128], bufs["excl"][0:8, 0:128],
                                   float(CAP), None, op0=ALU.is_lt)
        p.emit(DVE, fk, reads=[(bufs["excl"], "")], writes=[(bufs["excl"], "")])
        def fcf(E, i=i):
            return E.tensor_tensor(coef[0:8, i * 128:(i + 1) * 128],
                                   bufs["fg_e"][0:8, 0:128],
                                   bufs["excl"][0:8, 0:128], op=ALU.mult)
        p.emit(DVE, fcf, reads=[(bufs["fg_e"], ""), (bufs["excl"], "")],
               writes=[(coef, "t%d" % i)])
        def fcb(E, i=i):
            return E.tensor_copy(coef_bf[0:8, i * 128:(i + 1) * 128],
                                 coef[0:8, i * 128:(i + 1) * 128])
        p.emit(DVE, fcb, reads=[(coef, "t%d" % i)], writes=[(coef_bf, "t%d" % i)])


def emit_moe(ctx, xn, x_f32, T, x_row, x_off, coef, coef_bf, coef_dram,
             w_dram, b_moe_bf, bufs, load_queue=QPOOL, xn_row=None, xn_off=0):
    p, c = ctx.p, ctx.c
    xn_row = xn_row or T
    p.dma(QSP, lambda E: E.dma_start(out=coef_dram[:, :], in_=coef[0:8, 0:T]),
          reads=[(coef, "")], writes=[(coef_dram, "")])
    nt = T // 512
    we = bufs["we"]
    def mk_load(e, buf):
        src = w_dram[e].rearrange("(i p) o -> p i o", p=128)
        dst = buf.t[:, :].rearrange("p (i o) -> p i o", o=1024)
        p.dma(load_queue, lambda E: E.dma_start(out=dst, in_=src),
              writes=[(buf, "")])
    for tt in range(nt):
        t0 = tt * 512
        for e in range(8):
            def fr(E, e=e, t0=t0):
                return E.dma_start(out=bufs["crow"][0:1, 0:512],
                                   in_=coef_dram[e:e + 1, t0:t0 + 512])
            p.dma(QSP, fr, reads=[(coef_dram, "")], writes=[(bufs["crow"], "")])
            matmul_acc(p, bufs["pbx"][0:128, 0:512],
                       [(c["ones_f32_1x128"][0:1, :], bufs["crow"][0:1, 0:512])],
                       bufs["pbx"], reads=[(bufs["crow"], ""), (c["ones_f32_1x128"], "")])
            def fb(E, e=e):
                return E.tensor_copy(bufs["cbc"][:, e * 512:(e + 1) * 512],
                                     bufs["pbx"][0:128, 0:512])
            p.emit(DVE, fb, reads=[(bufs["pbx"], "")],
                   writes=[(bufs["cbc"], "e%d" % e)])
        for e in range(8):
            if tt == 0 and e == 0:
                mk_load(0, we[0])
            if e + 1 < 8 or tt + 1 < nt:
                mk_load((e + 1) % 8, we[(e + 1) % 2])
            web = we[e % 2]
            for ji in range(8):
                def fx(E, ji=ji, t0=t0, e=e):
                    return E.tensor_tensor(
                        bufs["xne"][:, ji * 512:(ji + 1) * 512],
                        xn[:, ji * xn_row + xn_off + t0: ji * xn_row + xn_off + t0 + 512],
                        bufs["cbc"][:, e * 512:(e + 1) * 512], op=ALU.mult)
                p.emit(DVE, fx, reads=[(xn, ""), (bufs["cbc"], "e%d" % e)],
                       writes=[(bufs["xne"], "j%d" % ji)])
            for jo in range(8):
                pairs = [(web[:, ji * 1024 + jo * 128: ji * 1024 + jo * 128 + 128],
                          bufs["xne"][:, ji * 512:(ji + 1) * 512])
                         for ji in range(8)]
                matmul_acc(p, bufs["pb"][jo][0:128, 0:512], pairs, bufs["pb"][jo],
                           reads=[(web, ""), (bufs["xne"], "")],
                           start=(e == 0), stop=False)
        for jo in range(8):
            matmul_acc(p, bufs["pb"][jo][0:128, 0:512],
                       [(b_moe_bf[0:8, jo * 128:(jo + 1) * 128],
                         coef_bf[0:8, t0:t0 + 512])],
                       bufs["pb"][jo], reads=[(b_moe_bf, ""), (coef_bf, "")],
                       start=False, stop=True)
            def fadd(E, jo=jo, t0=t0):
                a = x_f32[:, jo * x_row + x_off + t0: jo * x_row + x_off + t0 + 512]
                return E.tensor_tensor(a, a, bufs["pb"][jo][0:128, 0:512], op=ALU.add)
            p.emit(DVE, fadd, reads=[(bufs["pb"][jo], ""), (x_f32, "")],
                   writes=[(x_f32, "moe%d" % jo)])



def routing_bufs(p, pb, pbt):
    rb = dict(
        pb_r0=pb[4], pb_r1=pb[5], pb_rt=pb[6],
        pb_rtb=p.alias(pbt, pbt.t[:, :], "pb_rtb"),
        rstd_col=p.sbuf("rstd_col", [128, 1], F32),
        negcc=p.sbuf("negcc", [8, 1], F32),
        mrow=p.sbuf("mrow", [1, 128], F32),
        zrow=p.sbuf("zrow", [8, 128], F32),
        zc=p.sbuf("zc", [8, 128], F32),
        zt=p.sbuf("zt", [128, 8], F32),
        mx=p.sbuf("mx", [128, 1], F32),
        nmx=p.sbuf("nmx", [128, 1], F32),
        mm=p.sbuf("mm", [128, 8], F32),
        incl=p.sbuf("incl", [128, 8], F32),
        first=p.sbuf("first", [128, 8], F32),
        first_bf=p.sbuf("first_bf", [128, 8], BF16),
        ones8=p.sbuf("ones8", [128, 128], F32),
        lg=p.sbuf("lg", [128, 8], F32),
        ex=p.sbuf("ex", [128, 8], F32),
        sm=p.sbuf("sm", [128, 1], F32),
        fe=p.sbuf("fe", [128, 8], F32),
        gs=p.sbuf("gs", [128, 1], F32),
        rcp=p.sbuf("rcp", [128, 1], F32),
        gate=p.sbuf("gate", [128, 1], F32),
        fg_bf=p.sbuf("fg_bf", [128, 8], BF16),
        fe_e=p.sbuf("fe_e", [8, 128], F32),
        fg_e=p.sbuf("fg_e", [8, 128], F32),
        incl_e=p.sbuf("incl_e", [8, 128], F32),
        excl=p.sbuf("excl", [8, 128], F32),
        carry=p.sbuf("carry", [8, 1], F32),
    )
    def fms(E):
        return E.memset(rb["ones8"][:, :], 1.0)
    p.emit(DVE, fms, writes=[(rb["ones8"], "")])
    return rb


def build_launch1():
    nc = bass.Bass("TRN2", target_bir_lowering=False)
    x_in = nc.dram_tensor("x_in", [128, 8 * T_CORE], F32, kind="ExternalInput").ap()
    wg_in = nc.dram_tensor("wg_in", [1024, 8], F32, kind="ExternalInput").ap()
    cnt_d = nc.dram_tensor("cnt_out", [8, 1], F32, kind="ExternalOutput").ap()
    stack = ExitStack()
    with stack:
        p = Prog(nc, stack)
        consts = setup_consts(p, nc)
        x_f32 = p.sbuf("x_f32", [128, 8 * T_CORE], F32)
        p.dma(QSP, lambda E: E.dma_start(out=x_f32[:, :], in_=x_in[:, :]),
              writes=[(x_f32, "")])
        wg_f32 = p.sbuf("wg_f32", [128, 8 * 8], F32)
        src = wg_in.rearrange("(i p) o -> p i o", p=128)
        dst = wg_f32.t[:, :].rearrange("p (i o) -> p i o", o=8)
        p.dma(QSP, lambda E: E.dma_start(out=dst, in_=src), writes=[(wg_f32, "")])
        pb = [p.psum("pb%d" % i, [128, 512], F32) for i in range(7)]
        pbt = p.psum("pbt", [128, 1024], BF16)
        ctx = Ctx(p, consts)
        rb = routing_bufs(p, pb, pbt)
        emit_wg_prep(ctx, wg_f32, rb)
        cnts = p.sbuf("cnts", [8, 1], F32)
        for i in range(T_CORE // 128):
            emit_zc_tile(ctx, x_f32, T_CORE, i * 128, wg_f32, rb)
            emit_first_tile(ctx, rb)
            matmul_acc(p, pb[3][0:8, 0:1],
                       [(rb["first_bf"][0:128, 0:8], consts["ones_bf_128x1"][:, 0:1])],
                       pb[3], reads=[(rb["first_bf"], ""), (consts["ones_bf_128x1"], "")],
                       start=(i == 0), stop=(i == T_CORE // 128 - 1))
        def fc(E):
            return E.tensor_copy(cnts[0:8, 0:1], pb[3][0:8, 0:1])
        p.emit(DVE, fc, reads=[(pb[3], "")], writes=[(cnts, "")])
        p.dma(QSP, lambda E: E.dma_start(out=cnt_d[:, :], in_=cnts[0:8, 0:1]),
              reads=[(cnts, "")])
        p.wait_all_dma()
        with nc.Block() as block:
            p.finalize(block)
    return nc


def make_scr(p):
    return dict(
        tmp=p.sbuf("stmp", [128, 512], F32),
        ra=p.sbuf("sra", [1, 512], F32),
        rb=p.sbuf("srb", [1, 512], F32),
        rc=p.sbuf("src_", [1, 512], F32),
        sq=p.sbuf("ssq", [128, 512], BF16),
        b1=p.sbuf("sb1", [1, 1], F32),
        b2=p.sbuf("sb2", [1, 1], F32),
        b1_bf=p.sbuf("sb1bf", [1, 1], BF16),
        bias_col=p.sbuf("sbias", [128, 1], F32),
        recip=p.sbuf("srecip", [1, 512], F32),
        rln=p.sbuf("srln", [1, 512], F32),
        nbc=p.sbuf("snbc", [64, 512], F32),
        vtmp_bf=p.sbuf("svtmpbf", [64, 512], BF16),
        nmax_t=p.sbuf("snmaxt", [128, 1], F32),
        nmax=p.sbuf("snmax", [128, 1], F32),
        nmax_bf=p.sbuf("snmaxbf", [128, 1], BF16),
        nrm1=p.sbuf("snrm1", [1, 8], F32),
    )


def build_launch2():
    T = T_CORE
    Text = T + HALO
    nc = bass.Bass("TRN2", target_bir_lowering=False)
    di = {}
    def inp(name, shape, dtype=F32):
        di[name] = nc.dram_tensor(name, shape, dtype, kind="ExternalInput").ap()
        return di[name]
    inp("x_in", [128, 8 * Text])
    inp("wg_in", [1024, 8])
    for n in ("w_q0", "w_o0"): inp(n, [1024, 1024])
    for n in ("w_k0", "w_v0"): inp(n, [1024, 256])
    inp("w_moe", [8, 1024, 1024])
    inp("b_moe", [8, 1024])
    inp("b_q0", [128, 8]); inp("b_kv0", [128, 4]); inp("b_o0", [128, 8])
    inp("masks_in", [128, 2048], BF16)
    inp("vcol_in", [128, 4])
    inp("base_mask", [8, 8])
    inp("cnt_all", [8, 8])
    for n in ("w_qa", "w_ka", "w_va"): inp(n, [1024, 1024])
    for n in ("b_qa", "b_ka", "b_va"): inp(n, [128, 8])
    for n in ("w_qfT", "w_kfT", "w_vfT"): inp(n, [1024, 1024])
    for n in ("b_qf", "b_kf", "b_vf"): inp(n, [128, 8])
    inp("temp_in", [1, 16])
    inp("qe_in", [64, 16])
    do = {}
    def outp(name, shape, dtype=BF16):
        do[name] = nc.dram_tensor(name, shape, dtype, kind="ExternalOutput").ap()
        return do[name]
    outp("x1_out", [128, 8 * T], F32)
    for n in ("qa_out", "ka_out", "va_out", "qf_out", "kf_out", "vf_out"):
        outp(n, [128, 8 * T])
    coef_dram = nc.dram_tensor("coef_dram", [8, T], F32, kind="Internal").ap()

    stack = ExitStack()
    with stack:
        p = Prog(nc, stack)
        consts = setup_consts(p, nc)
        c = consts
        ctx = Ctx(p, consts)
        x_own = p.sbuf("x_own", [128, 8 * T], F32)
        xn = p.sbuf("xn", [128, 8 * Text], BF16)
        rstd_row = p.sbuf("rstd_row", [1, T], BF16)
        scr = make_scr(p)
        pb = [p.psum("pb%d" % i, [128, 512], F32) for i in range(7)]
        pbt = p.psum("pbt", [128, 1024], BF16)
        pb7f = p.alias(pbt, pbt.t[:, :].bitcast(F32), "pb7f")
        pb8 = pb + [pb7f]
        masks = p.sbuf("masks", [128, 2048], BF16)
        p.dma(QSP, lambda E: E.dma_start(out=masks[:, :], in_=di["masks_in"][:, :]),
              writes=[(masks, "")])
        small = {}
        for n in ("b_q0", "b_kv0", "b_o0", "vcol_in", "base_mask", "cnt_all",
                  "b_qa", "b_ka", "b_va", "b_qf", "b_kf", "b_vf",
                  "temp_in", "qe_in"):
            sb = p.sbuf("s_" + n, list(di[n].shape), F32)
            p.dma(QSP, lambda E, sb=sb, n=n: E.dma_start(out=sb[:, :], in_=di[n][:, :]),
                  writes=[(sb, "")])
            small[n] = sb
        wg_f32 = p.sbuf("wg_f32", [128, 8 * 8], F32)
        srcwg = di["wg_in"].rearrange("(i p) o -> p i o", p=128)
        dstwg = wg_f32.t[:, :].rearrange("p (i o) -> p i o", o=8)
        p.dma(QSP, lambda E: E.dma_start(out=dstwg, in_=srcwg), writes=[(wg_f32, "")])
        b_moe_bf = p.sbuf("b_moe_bf", [8, 1024], BF16)
        p.dma(QPOOL, lambda E: E.dma_start(out=b_moe_bf[:, :], in_=di["b_moe"][:, :]),
              writes=[(b_moe_bf, "")])

        sub = ExitStack()
        with sub:
            x_halo = p.sbuf("x_halo", [128, 8 * HALO], F32, stack=sub)
            src_h = di["x_in"].rearrange("p (j t) -> p j t", t=Text)
            p.dma(QSP, lambda E: E.dma_start(
                out=x_halo.t[:, :].rearrange("p (j t) -> p j t", t=HALO),
                in_=src_h[:, :, 0:HALO]), writes=[(x_halo, "")])
            p.dma(QSP, lambda E: E.dma_start(
                out=x_own.t[:, :].rearrange("p (j t) -> p j t", t=T),
                in_=src_h[:, :, HALO:Text]), writes=[(x_own, "")])
            emit_ln(ctx, x_halo, HALO, xn, pb, scr, None, xn_row=Text, xn_off=0)
            emit_ln(ctx, x_own, T, xn, pb, scr, rstd_row, xn_row=Text, xn_off=HALO)
            p.barrier()
        sub = ExitStack()
        with sub:
            wqh = p.sbuf("wqh", [128, 8 * 512], BF16, stack=sub)
            def wq_loader(half):
                src = di["w_q0"].rearrange("(i p) o -> p i o", p=128)[:, :, half * 512:(half + 1) * 512]
                dst = wqh.t[:, :].rearrange("p (i o) -> p i o", o=512)
                p.dma(QPOOL, lambda E: E.dma_start(out=dst, in_=src), writes=[(wqh, "")])
            wk = load_weight(p, nc, "wk0", di["w_k0"], 1024, 256, stack=sub)
            wv = load_weight(p, nc, "wv0", di["w_v0"], 1024, 256, stack=sub)
            wo = load_weight(p, nc, "wo0", di["w_o0"], 1024, 1024, stack=sub)
            bufs = dict(
                kg=[p.sbuf("kg%d" % g, [64, Text], BF16, stack=sub) for g in range(4)],
                vtg=[p.sbuf("vtg%d" % g, [128, 20 * 65], BF16, stack=sub) for g in range(4)],
                q=p.sbuf("qh", [64, 512], BF16, stack=sub),
                out_cat=p.sbuf("out_cat", [128, 8 * 512], BF16, stack=sub),
                attT=[p.sbuf("attT%d" % i, [128, 512], BF16, stack=sub) for i in range(2)],
                scr=scr, pb=pb, pbt=pbt,
                nrm=p.sbuf("nrm", [1, 64], F32, stack=sub),
                x_row=T, x_off=0,
            )
            emit_l0_attn(ctx, xn, x_own, None,
                         dict(wq=wqh, wq_loader=wq_loader, wk=wk, wv=wv, wo=wo,
                              bq=small["b_q0"], bkv=small["b_kv0"], bo=small["b_o0"]),
                         masks, small["vcol_in"], bufs)
            p.barrier()
        sub = ExitStack()
        with sub:
            rb = routing_bufs(p, pb, pbt)
            emit_wg_prep(ctx, wg_f32, rb)
            base_col = p.sbuf("base_col", [8, 1], F32, stack=sub)
            tmp88 = p.sbuf("tmp88", [8, 8], F32, stack=sub)
            def fb0(E):
                return E.tensor_tensor(tmp88[0:8, 0:8], small["cnt_all"][0:8, 0:8],
                                       small["base_mask"][0:8, 0:8], op=ALU.mult)
            p.emit(DVE, fb0, reads=[(small["cnt_all"], ""), (small["base_mask"], "")],
                   writes=[(tmp88, "")])
            def fb1(E):
                return E.tensor_reduce(base_col[0:8, 0:1], tmp88[0:8, 0:8],
                                       axis=AX.X, op=ALU.add)
            p.emit(DVE, fb1, reads=[(tmp88, "")], writes=[(base_col, "")])
            coef = p.sbuf("coef", [8, T], F32, stack=sub)
            coef_bf = p.sbuf("coef_bf", [8, T], BF16, stack=sub)
            xtile = p.sbuf("xtile", [128, 8 * 128], F32, stack=sub)
            src_x = di["x_in"].rearrange("p (j t) -> p j t", t=Text)
            def tile_loader(i):
                def fld(E, i=i):
                    dst = xtile.t[:, :].rearrange("p (j t) -> p j t", t=128)
                    return E.dma_start(out=dst,
                                       in_=src_x[:, :, HALO + i * 128: HALO + (i + 1) * 128])
                p.dma(QSP, fld, writes=[(xtile, "")])
                return xtile, 128, 0
            emit_routing_own(ctx, x_own, T, 0, T, rstd_row, wg_f32, base_col,
                             rb, coef, coef_bf, tile_loader=tile_loader)
            moe_bufs = dict(
                we=[p.sbuf("we%d" % i, [128, 8 * 1024], BF16, stack=sub) for i in range(2)],
                cbc=p.sbuf("cbc", [128, 8 * 512], BF16, stack=sub),
                crow=p.sbuf("crow", [1, 512], F32, stack=sub),
                xne=p.sbuf("xne", [128, 8 * 512], BF16, stack=sub),
                pb=pb8, pbx=pb[4],
            )
            emit_moe(ctx, xn, x_own, T, T, 0, coef, coef_bf, Buf("coef_dram", coef_dram),
                     di["w_moe"], b_moe_bf, moe_bufs, xn_row=Text, xn_off=HALO)
            p.barrier()
        p.dma(QSP, lambda E: E.dma_start(out=do["x1_out"][:, :], in_=x_own[:, :]),
              reads=[(x_own, "")])
        emit_ln(ctx, x_own, T, xn, pb, scr, None, xn_row=Text, xn_off=0)
        sub = ExitStack()
        with sub:
            wbuf = p.sbuf("wbuf", [128, 8 * 1024], BF16, stack=sub)
            stage = [p.sbuf("stage%d" % i, [128, 512], BF16, stack=sub) for i in range(2)]
            def project_out(wname, bname, outname):
                src = di[wname].rearrange("(i p) o -> p i o", p=128)
                dst = wbuf.t[:, :].rearrange("p (i o) -> p i o", o=1024)
                p.dma(QPOOL, lambda E: E.dma_start(out=dst, in_=src), writes=[(wbuf, "")])
                k = [0]
                def writer(jo, t0, t1, pbk, bias_ap):
                    st = stage[k[0] % 2]
                    k[0] += 1
                    def fev(E, st=st, pbk=pbk, bias_ap=bias_ap):
                        return E.activation(st[:, 0:t1 - t0], pbk[0:128, 0:t1 - t0],
                                            AF.Identity, bias=bias_ap, scale=1.0)
                    p.emit(ACT, fev, reads=[(pbk, ""), (small[bname], "")],
                           writes=[(st, "")])
                    def fdma(E, st=st, jo=jo, t0=t0, t1=t1):
                        return E.dma_start(out=do[outname][:, jo * T + t0: jo * T + t1],
                                           in_=st[:, 0:t1 - t0])
                    p.dma(QSP, fdma, reads=[(st, "")])
                emit_proj(ctx, wbuf, xn, Text, 1024, 1024, writer,
                          t_tiles=[(t0, t0 + 512) for t0 in range(0, T, 512)],
                          bias_sb=small[bname], pb_sel=[pb[0], pb[1]])
            project_out("w_qa", "b_qa", "qa_out")
            project_out("w_ka", "b_ka", "ka_out")
            project_out("w_va", "b_va", "va_out")
            project_out("w_kfT", "b_kf", "kf_out")
            project_out("w_vfT", "b_vf", "vf_out")
            sp_row = p.sbuf("sp_row", [1, 16], F32, stack=sub)
            sp8_row = p.sbuf("sp8_row", [1, 16], F32, stack=sub)
            lnsp8 = p.sbuf("lnsp8", [1, 16], F32, stack=sub)
            t_tmp = p.sbuf("t_tmp", [1, 16], F32, stack=sub)
            t_abs = p.sbuf("t_abs", [1, 16], F32, stack=sub)
            qec = p.sbuf("qec", [64, 16], F32, stack=sub)
            def s1(E):
                return E.tensor_scalar(t_tmp[0:1, 0:16], small["temp_in"][0:1, 0:16],
                                       0.0, None, op0=ALU.max)
            p.emit(DVE, s1, reads=[(small["temp_in"], "")], writes=[(t_tmp, "")])
            def s2(E):
                return E.activation(t_abs[0:1, 0:16], small["temp_in"][0:1, 0:16],
                                    AF.Abs, bias=0.0, scale=1.0)
            p.emit(ACT, s2, reads=[(small["temp_in"], "")], writes=[(t_abs, "")])
            def s3(E):
                return E.activation(t_abs[0:1, 0:16], t_abs[0:1, 0:16], AF.Exp,
                                    bias=0.0, scale=-1.0)
            p.emit(ACT, s3, reads=[(t_abs, "")], writes=[(t_abs, "")])
            def s4(E):
                return E.activation(t_abs[0:1, 0:16], t_abs[0:1, 0:16], AF.Ln,
                                    bias=c["one_f32_1x1"][0:1, 0:1], scale=1.0)
            p.emit(ACT, s4, reads=[(t_abs, ""), (c["one_f32_1x1"], "")],
                   writes=[(t_abs, "")])
            def s5(E):
                return E.tensor_tensor(sp_row[0:1, 0:16], t_tmp[0:1, 0:16],
                                       t_abs[0:1, 0:16], op=ALU.add)
            p.emit(DVE, s5, reads=[(t_tmp, ""), (t_abs, "")], writes=[(sp_row, "")])
            def s6(E):
                return E.tensor_scalar(sp8_row[0:1, 0:16], sp_row[0:1, 0:16],
                                       0.125, None, op0=ALU.mult)
            p.emit(DVE, s6, reads=[(sp_row, "")], writes=[(sp8_row, "")])
            def s7(E):
                return E.activation(lnsp8[0:1, 0:16], sp8_row[0:1, 0:16], AF.Ln,
                                    bias=0.0, scale=1.0)
            p.emit(ACT, s7, reads=[(sp8_row, "")], writes=[(lnsp8, "")])
            matmul_acc(p, pb[2][0:64, 0:16],
                       [(c["ones_f32_1x128"][0:1, 0:64], sp8_row[0:1, 0:16])],
                       pb[2], reads=[(sp8_row, ""), (c["ones_f32_1x128"], "")])
            def s8(E):
                return E.tensor_tensor(qec[0:64, 0:16], small["qe_in"][0:64, 0:16],
                                       pb[2][0:64, 0:16], op=ALU.mult)
            p.emit(DVE, s8, reads=[(small["qe_in"], ""), (pb[2], "")], writes=[(qec, "")])
            srcq = di["w_qfT"].rearrange("(i p) o -> p i o", p=128)
            dstq = wbuf.t[:, :].rearrange("p (i o) -> p i o", o=1024)
            p.dma(QPOOL, lambda E: E.dma_start(out=dstq, in_=srcq), writes=[(wbuf, "")])
            q_h = p.sbuf("q_h", [64, T], F32, stack=sub)
            q2 = p.sbuf("q2", [64, 512], BF16, stack=sub)
            nrm_row = p.sbuf("nrm_row", [1, T], F32, stack=sub)
            alpha = p.sbuf("alpha", [1, T], BF16, stack=sub)
            abc = p.sbuf("abc", [64, 512], F32, stack=sub)
            qf_tmp = p.sbuf("qf_tmp", [64, 512], F32, stack=sub)
            for h in range(16):
                for tt in range(4):
                    t0 = tt * 512
                    pairs = [(wbuf[:, ji * 1024 + h * 64: ji * 1024 + h * 64 + 64],
                              xn[:, ji * Text + t0: ji * Text + t0 + 512])
                             for ji in range(8)]
                    matmul_acc(p, pb[3][0:64, 0:512], pairs, pb[3],
                               reads=[(wbuf, ""), (xn, "")])
                    bias_ap = small["b_qf"][(h % 2) * 64:(h % 2) * 64 + 64,
                                            h // 2: h // 2 + 1]
                    def fq(E, t0=t0, bias_ap=bias_ap):
                        return E.activation(q_h[0:64, t0:t0 + 512], pb[3][0:64, 0:512],
                                            AF.Identity, bias=bias_ap, scale=1.0)
                    p.emit(ACT, fq, reads=[(pb[3], ""), (small["b_qf"], "")],
                           writes=[(q_h, "t%d" % tt)])
                    def fq2(E, t0=t0):
                        return E.tensor_tensor(q2[0:64, 0:512], q_h[0:64, t0:t0 + 512],
                                               q_h[0:64, t0:t0 + 512], op=ALU.mult)
                    p.emit(DVE, fq2, reads=[(q_h, "t%d" % tt)], writes=[(q2, "")])
                    matmul_acc(p, pb[4][0:1, 0:512],
                               [(c["ones_bf_64x1"][0:64, 0:1], q2[0:64, 0:512])],
                               pb[4], reads=[(q2, ""), (c["ones_bf_64x1"], "")])
                    def fn(E, t0=t0):
                        return E.tensor_copy(nrm_row[0:1, t0:t0 + 512],
                                             pb[4][0:1, 0:512])
                    p.emit(DVE, fn, reads=[(pb[4], "")], writes=[(nrm_row, "t%d" % tt)])
                for tt in range(4):
                    t0 = tt * 512
                    def fl(E, t0=t0):
                        return E.activation(scr["rc"][0:1, 0:512], nrm_row[0:1, t0:t0 + 512],
                                            AF.Ln, bias=0.0, scale=1.0)
                    p.emit(ACT, fl, reads=[(nrm_row, "t%d" % tt)], writes=[(scr["rc"], "")])
                    def fa(E, t0=t0, h=h):
                        return E.activation(alpha[0:1, t0:t0 + 512], scr["rc"][0:1, 0:512],
                                            AF.Exp, bias=lnsp8[0:1, h:h + 1], scale=-0.5)
                    p.emit(ACT, fa, reads=[(scr["rc"], ""), (lnsp8, "")],
                           writes=[(alpha, "t%d" % tt)])
                for tt in range(4):
                    t0 = tt * 512
                    matmul_acc(p, pb[5][0:64, 0:512],
                               [(c["ones_bf_1x128"][0:1, 0:64], alpha[0:1, t0:t0 + 512])],
                               pb[5], reads=[(alpha, "t%d" % tt), (c["ones_bf_1x128"], "")])
                    def fb2(E):
                        return E.tensor_copy(abc[0:64, 0:512], pb[5][0:64, 0:512])
                    p.emit(DVE, fb2, reads=[(pb[5], "")], writes=[(abc, "")])
                    def fm(E, t0=t0):
                        return E.tensor_tensor(qf_tmp[0:64, 0:512], q_h[0:64, t0:t0 + 512],
                                               abc[0:64, 0:512], op=ALU.mult)
                    p.emit(DVE, fm, reads=[(q_h, "t%d" % tt), (abc, "")],
                           writes=[(qf_tmp, "")])
                    st = stage[tt % 2]
                    def fadd(E, h=h, st=st):
                        return E.tensor_scalar(st[0:64, 0:512], qf_tmp[0:64, 0:512],
                                               qec[0:64, h:h + 1], None, op0=ALU.add)
                    p.emit(DVE, fadd, reads=[(qf_tmp, ""), (qec, "")], writes=[(st, "")])
                    def fdma(E, h=h, t0=t0, st=st):
                        jo, pbase = h // 2, (h % 2) * 64
                        return E.dma_start(
                            out=do["qf_out"][pbase:pbase + 64, jo * T + t0: jo * T + t0 + 512],
                            in_=st[0:64, 0:512])
                    p.dma(QSP, fdma, reads=[(st, "")])
        p.wait_all_dma()
        with nc.Block() as block:
            p.finalize(block)
    return nc


def build_launch3():
    T = T_CORE
    nc = bass.Bass("TRN2", target_bir_lowering=False)
    di = {}
    def inp(name, shape, dtype=BF16):
        di[name] = nc.dram_tensor(name, shape, dtype, kind="ExternalInput").ap()
        return di[name]
    inp("x1_in", [128, 8 * T], F32)
    inp("qa_in", [128, 8 * T])
    inp("ka_in", [128, 8 * T])
    inp("va_tok", [T, 1024])
    inp("kd_nb", [128, 8 * 1024])
    inp("vd_tok", [1024, 1024])
    inp("qf_in", [128, 8 * T])
    inp("kf_full", [128, 8 * SEQ])
    inp("vf_tok", [SEQ, 1024])
    inp("w_oa", [1024, 1024], F32)
    inp("b_oa", [128, 8], F32)
    inp("wvalid", [128, 16], F32)
    inp("w_rT", [1024, 15], F32)
    inp("b_r", [15, 1], F32)
    inp("w_sT", [1024, 2], F32)
    inp("b_s", [2, 1], F32)
    inp("w_pT", [1024, 1024], F32)
    inp("b_p", [128, 8], F32)
    xout_d = nc.dram_tensor("x_out", [128, 8 * T], F32, kind="ExternalOutput").ap()
    dbg_ha = nc.dram_tensor("dbg_ha", [128, 8 * T], BF16, kind="ExternalOutput").ap()
    dbg_hf = nc.dram_tensor("dbg_hf", [128, 8 * T], BF16, kind="ExternalOutput").ap()

    stack = ExitStack()
    with stack:
        p = Prog(nc, stack)
        consts = setup_consts(p, nc)
        c = consts
        ctx = Ctx(p, consts)
        scr = make_scr(p)
        pb = [p.psum("pb%d" % i, [128, 512], F32) for i in range(7)]
        pbt = p.psum("pbt", [128, 1024], BF16)
        x1 = p.sbuf("x1", [128, 8 * T], F32)
        p.dma(QSP, lambda E: E.dma_start(out=x1[:, :], in_=di["x1_in"][:, :]),
              writes=[(x1, "")])
        small = {}
        for n in ("b_oa", "wvalid", "b_r", "b_s", "b_p"):
            sb = p.sbuf("s_" + n, list(di[n].shape), F32)
            p.dma(QSP, lambda E, sb=sb, n=n: E.dma_start(out=sb[:, :], in_=di[n][:, :]),
                  writes=[(sb, "")])
            small[n] = sb
        h_f = p.sbuf("h_f", [128, 8 * T], BF16)
        attT = [p.sbuf("attT%d" % i, [128, 512], BF16) for i in range(2)]

        sub = ExitStack()
        with sub:
            kd = p.sbuf("kd", [128, 8 * 1024], BF16, stack=sub)
            p.dma(QSP, lambda E: E.dma_start(out=kd[:, :], in_=di["kd_nb"][:, :]),
                  writes=[(kd, "")])
            qa_seg = p.sbuf("qa_seg", [128, 8 * 512], BF16, stack=sub)
            ka_seg = p.sbuf("ka_seg", [128, 8 * 512], BF16, stack=sub)
            vt_own = p.sbuf("vt_own", [128, 16 * 65], BF16, stack=sub)
            vt_nb = p.sbuf("vt_nb", [128, 8 * 65], BF16, stack=sub)
            acc = p.sbuf("acc", [128, 4 * 64], F32, stack=sub)
            acc_bf = p.sbuf("acc_bf", [128, 4 * 64], BF16, stack=sub)
            cr_tmp = p.sbuf("cr_tmp", [128, 64], F32, stack=sub)
            rcol = p.sbuf("rcol", [128, 1], F32, stack=sub)
            for h in range(16):
                pbase, jo = (h % 2) * 64, h // 2
                def fm1(E):
                    return E.memset(vt_own[:, :], 1.0)
                p.emit(DVE, fm1, writes=[(vt_own, "")])
                def fm2(E):
                    return E.memset(vt_nb[:, :], 1.0)
                p.emit(DVE, fm2, writes=[(vt_nb, "")])
                def fv1(E, h=h):
                    dst = vt_own.t[:, :].rearrange("p (k o) -> p k o", o=65)[:, :, 0:64]
                    src = di["va_tok"][:, h * 64:(h + 1) * 64].rearrange(
                        "(k p) d -> p k d", p=128)
                    return E.dma_start(out=dst, in_=src)
                p.dma(QSP, fv1, writes=[(vt_own, "")])
                def fv2(E, h=h):
                    dst = vt_nb.t[:, :].rearrange("p (k o) -> p k o", o=65)[:, :, 0:64]
                    src = di["vd_tok"][:, h * 64:(h + 1) * 64].rearrange(
                        "(k p) d -> p k d", p=128)
                    return E.dma_start(out=dst, in_=src)
                p.dma(QSP, fv2, writes=[(vt_nb, "")])
                kd_ap_all = kd.t[pbase:pbase + 64, jo * 1024: (jo + 1) * 1024]
                emit_norm_scalar(ctx, kd_ap_all, [(kd, "")], 1024, scr["nrm1"], 2,
                                 scr, pb[2])
                for seg in range(4):
                    def fq(E, seg=seg):
                        dst = qa_seg.t[:, :].rearrange("p (j t) -> p j t", t=512)
                        src = di["qa_in"].rearrange("p (j t) -> p j t", t=T)[:, :, seg * 512:(seg + 1) * 512]
                        return E.dma_start(out=dst, in_=src)
                    p.dma(QSP, fq, writes=[(qa_seg, "")])
                    def fk2(E, seg=seg):
                        dst = ka_seg.t[:, :].rearrange("p (j t) -> p j t", t=512)
                        src = di["ka_in"].rearrange("p (j t) -> p j t", t=T)[:, :, seg * 512:(seg + 1) * 512]
                        return E.dma_start(out=dst, in_=src)
                    p.dma(QSP, fk2, writes=[(ka_seg, "")])
                    q_ap = qa_seg[pbase:pbase + 64, jo * 512:(jo + 1) * 512]
                    q_reads = [(qa_seg, "")]
                    emit_norm_scalar(ctx, qa_seg.t[pbase:pbase + 64, jo * 512:(jo + 1) * 512],
                                     q_reads, 512, scr["nrm1"], 0, scr, pb[2])
                    emit_norm_scalar(ctx, ka_seg.t[pbase:pbase + 64, jo * 512:(jo + 1) * 512],
                                     [(ka_seg, "")], 512, scr["nrm1"], 1, scr, pb[2])
                    def fmrg(E):
                        return E.tensor_tensor(scr["nrm1"][0:1, 1:2], scr["nrm1"][0:1, 1:2],
                                               scr["nrm1"][0:1, 2:3], op=ALU.max)
                    p.emit(DVE, fmrg, reads=[(scr["nrm1"], "i1"), (scr["nrm1"], "i2")],
                           writes=[(scr["nrm1"], "i1")])
                    emit_bias_col(ctx, scr["nrm1"], scr["nrm1"], 0, 1, 0.125,
                                  scr["bias_col"], [pb[2]], scr)
                    for i in range(4):
                        sb = pb[[0, 1][i % 2]]
                        at = attT[i % 2]
                        matmul_acc(p, sb[0:128, 0:512],
                                   [(ka_seg[pbase:pbase + 64, jo * 512 + i * 128: jo * 512 + i * 128 + 128],
                                     q_ap)], sb, reads=[(ka_seg, "")] + q_reads)
                        def fexp(E, sb=sb, at=at):
                            return E.activation(at[0:128, 0:512], sb[0:128, 0:512], AF.Exp,
                                                bias=scr["bias_col"][0:128, 0:1], scale=0.125)
                        p.emit(ACT, fexp, reads=[(sb, ""), (scr["bias_col"], "")],
                               writes=[(at, "")])
                        for sq in range(4):
                            ci = seg * 4 + i
                            matmul_acc(p, pb[2 + sq][0:128, 0:65],
                                       [(at[0:128, sq * 128:(sq + 1) * 128],
                                         vt_own[:, ci * 65:(ci + 1) * 65])],
                                       pb[2 + sq], reads=[(at, ""), (vt_own, "")],
                                       start=(i == 0), stop=(i == 3))
                    for sq in range(4):
                        def fr1(E, sq=sq):
                            return E.reciprocal(rcol[0:128, 0:1],
                                                pb[2 + sq][0:128, 64:65])
                        p.emit(DVE, fr1, reads=[(pb[2 + sq], "")], writes=[(rcol, "")])
                        def fr2(E, sq=sq):
                            return E.tensor_scalar(acc[0:128, sq * 64:(sq + 1) * 64],
                                                   pb[2 + sq][0:128, 0:64],
                                                   rcol[0:128, 0:1], None, op0=ALU.mult)
                        p.emit(DVE, fr2, reads=[(pb[2 + sq], ""), (rcol, "")],
                               writes=[(acc, "s%d" % sq)])
                    for oi, off in enumerate((-2, -1, 1, 2)):
                        nb = seg + off + 2
                        sb = pb[[0, 1][oi % 2]]
                        at = attT[oi % 2]
                        matmul_acc(p, sb[0:128, 0:512],
                                   [(kd[pbase:pbase + 64, jo * 1024 + nb * 128: jo * 1024 + nb * 128 + 128],
                                     q_ap)], sb, reads=[(kd, "")] + q_reads)
                        def fexp2(E, sb=sb, at=at):
                            return E.activation(at[0:128, 0:512], sb[0:128, 0:512], AF.Exp,
                                                bias=scr["bias_col"][0:128, 0:1], scale=0.125)
                        p.emit(ACT, fexp2, reads=[(sb, ""), (scr["bias_col"], "")],
                               writes=[(at, "")])
                        for sq in range(4):
                            matmul_acc(p, pb[6][0:128, 0:65],
                                       [(at[0:128, sq * 128:(sq + 1) * 128],
                                         vt_nb[:, nb * 65:(nb + 1) * 65])],
                                       pb[6], reads=[(at, ""), (vt_nb, "")])
                            def fr3(E):
                                return E.reciprocal(rcol[0:128, 0:1], pb[6][0:128, 64:65])
                            p.emit(DVE, fr3, reads=[(pb[6], "")], writes=[(rcol, "")])
                            def fr4(E):
                                return E.tensor_scalar(cr_tmp[0:128, 0:64],
                                                       pb[6][0:128, 0:64],
                                                       rcol[0:128, 0:1], None, op0=ALU.mult)
                            p.emit(DVE, fr4, reads=[(pb[6], ""), (rcol, "")],
                                   writes=[(cr_tmp, "")])
                            def fr5(E, seg=seg, oi=oi, sq=sq):
                                return E.scalar_tensor_tensor(
                                    acc[0:128, sq * 64:(sq + 1) * 64],
                                    cr_tmp[0:128, 0:64],
                                    small["wvalid"][0:128, seg * 4 + oi: seg * 4 + oi + 1],
                                    acc[0:128, sq * 64:(sq + 1) * 64],
                                    op0=ALU.mult, op1=ALU.add)
                            p.emit(DVE, fr5, reads=[(cr_tmp, ""), (small["wvalid"], ""),
                                                    (acc, "s%d" % sq)],
                                   writes=[(acc, "s%d" % sq)])
                    def fcb(E):
                        return E.tensor_copy(acc_bf[:, :], acc[:, :])
                    p.emit(DVE, fcb, reads=[(acc, "")], writes=[(acc_bf, "")])
                    for sq in range(4):
                        def ftr(E, sq=sq):
                            return E.transpose(pbt[0:64, 0:128],
                                               acc_bf[0:128, sq * 64:(sq + 1) * 64],
                                               c["ident"][:, :])
                        p.emit(PE, ftr, reads=[(acc_bf, ""), (c["ident"], "")],
                               writes=[(pbt, "")])
                        def fcp(E, sq=sq, seg=seg, jo=jo, pbase=pbase):
                            col = jo * T + seg * 512 + sq * 128
                            return E.tensor_copy(h_f[pbase:pbase + 64, col:col + 128],
                                                 pbt[0:64, 0:128])
                        p.emit(DVE, fcp, reads=[(pbt, "")],
                               writes=[(h_f, "h%d_%d_%d" % (h, seg, sq))])
            p.dma(QSP, lambda E: E.dma_start(out=dbg_ha[:, :], in_=h_f[:, :]),
                  reads=[(h_f, "")])
            wo = load_weight(p, nc, "woa", di["w_oa"], 1024, 1024, stack=sub)
            for tt in range(4):
                t0 = tt * 512
                for jo2 in range(8):
                    pairs = [(wo[:, ji * 1024 + jo2 * 128: ji * 1024 + jo2 * 128 + 128],
                              h_f[:, ji * T + t0: ji * T + t0 + 512]) for ji in range(8)]
                    matmul_acc(p, pb[0][0:128, 0:512], pairs, pb[0],
                               reads=[(wo, ""), (h_f, "")])
                    def fadd(E, jo2=jo2, t0=t0):
                        a = x1[:, jo2 * T + t0: jo2 * T + t0 + 512]
                        return E.tensor_tensor(a, a, pb[0][0:128, 0:512], op=ALU.add)
                    p.emit(DVE, fadd, reads=[(pb[0], ""), (x1, "")],
                           writes=[(x1, "c%d" % jo2)])
            for jo2 in range(8):
                for tt in range(4):
                    t0 = tt * 512
                    def fb(E, jo2=jo2, t0=t0):
                        a = x1[:, jo2 * T + t0: jo2 * T + t0 + 512]
                        return E.tensor_scalar(a, a, small["b_oa"][:, jo2:jo2 + 1],
                                               None, op0=ALU.add)
                    p.emit(DVE, fb, reads=[(x1, ""), (small["b_oa"], "")],
                           writes=[(x1, "c%d" % jo2)])
            p.barrier()
        sub = ExitStack()
        with sub:
            qf = p.sbuf("qf", [128, 8 * T], BF16, stack=sub)
            p.dma(QSP, lambda E: E.dma_start(out=qf[:, :], in_=di["qf_in"][:, :]),
                  writes=[(qf, "")])
            kf_sb = p.sbuf("kf_sb", [128, SEQ], BF16, stack=sub)
            vt_f = p.sbuf("vt_f", [128, 64 * 65], BF16, stack=sub)
            for h in range(16):
                pbase, jo = (h % 2) * 64, h // 2
                def fk(E, h=h, pbase=pbase, jo=jo):
                    return E.dma_start(out=kf_sb[pbase:pbase + 64, :],
                                       in_=di["kf_full"][pbase:pbase + 64,
                                                         jo * SEQ:(jo + 1) * SEQ])
                p.dma(QSP, fk, writes=[(kf_sb, "")])
                def fmv(E):
                    return E.memset(vt_f[:, :], 1.0)
                p.emit(DVE, fmv, writes=[(vt_f, "")])
                def fv(E, h=h):
                    dst = vt_f.t[:, :].rearrange("p (k o) -> p k o", o=65)[:, :, 0:64]
                    src = di["vf_tok"][:, h * 64:(h + 1) * 64].rearrange(
                        "(k p) d -> p k d", p=128)
                    return E.dma_start(out=dst, in_=src)
                p.dma(QSP, fv, writes=[(vt_f, "")])
                emit_norm_scalar(ctx, kf_sb.t[pbase:pbase + 64, :], [(kf_sb, "")], SEQ,
                                 scr["nrm1"], 1, scr, pb[2])
                for qt in range(4):
                    q_ap = qf[pbase:pbase + 64, jo * T + qt * 512: jo * T + qt * 512 + 512]
                    emit_norm_scalar(ctx, qf.t[pbase:pbase + 64,
                                               jo * T + qt * 512: jo * T + qt * 512 + 512],
                                     [(qf, "")], 512, scr["nrm1"], 0, scr, pb[2])
                    emit_bias_col(ctx, scr["nrm1"], scr["nrm1"], 0, 1, 1.0,
                                  scr["bias_col"], [pb[2]], scr)
                    chunks = []
                    for kc in range(64):
                        chunks.append(dict(
                            k_ap=kf_sb[pbase:pbase + 64, kc * 128:(kc + 1) * 128],
                            k_reads=[(kf_sb, "")],
                            vt_ap=vt_f[:, kc * 65:(kc + 1) * 65],
                            vt_reads=[(vt_f, "")],
                        ))
                    def out_fn(ps_av, nbc, h=h, qt=qt, pbase=pbase, jo=jo):
                        def fo(E):
                            return E.tensor_tensor(
                                h_f[pbase:pbase + 64, jo * T + qt * 512: jo * T + qt * 512 + 512],
                                ps_av[0:64, 0:512], nbc[0:64, 0:512], op=ALU.mult)
                        p.emit(DVE, fo, reads=[(ps_av, ""), (nbc, "")],
                               writes=[(h_f, "h%d_%d" % (h, qt))])
                    emit_attn_unit(ctx, q_ap, [(qf, "")], chunks, scr["bias_col"],
                                   1.0, 512, [pb[3], pb[4]], pb[5], pb[6], attT,
                                   scr, out_fn)
            p.dma(QSP, lambda E: E.dma_start(out=dbg_hf[:, :], in_=h_f[:, :]),
                  reads=[(h_f, "")])
            p.barrier()
        sub = ExitStack()
        with sub:
            wr = p.sbuf("wr", [128, 8 * 15], BF16, stack=sub)
            srcr = di["w_rT"].rearrange("(i p) o -> p i o", p=128)
            p.dma(QPOOL, lambda E: E.dma_start(
                out=wr.t[:, :].rearrange("p (i o) -> p i o", o=15), in_=srcr),
                writes=[(wr, "")])
            ws = p.sbuf("ws", [128, 8 * 2], BF16, stack=sub)
            srcs = di["w_sT"].rearrange("(i p) o -> p i o", p=128)
            p.dma(QPOOL, lambda E: E.dma_start(
                out=ws.t[:, :].rearrange("p (i o) -> p i o", o=2), in_=srcs),
                writes=[(ws, "")])
            rlog = p.sbuf("rlog", [15, T], F32, stack=sub)
            slog = p.sbuf("slog", [2, T], F32, stack=sub)
            for tt in range(4):
                t0 = tt * 512
                pairs = [(wr[:, ji * 15: ji * 15 + 15],
                          h_f[:, ji * T + t0: ji * T + t0 + 512]) for ji in range(8)]
                matmul_acc(p, pb[0][0:15, 0:512], pairs, pb[0],
                           reads=[(wr, ""), (h_f, "")])
                def fev(E, t0=t0):
                    return E.activation(rlog[0:15, t0:t0 + 512], pb[0][0:15, 0:512],
                                        AF.Identity, bias=small["b_r"][0:15, 0:1],
                                        scale=1.0)
                p.emit(ACT, fev, reads=[(pb[0], ""), (small["b_r"], "")],
                       writes=[(rlog, "t%d" % tt)])
                pairs = [(ws[:, ji * 2: ji * 2 + 2],
                          h_f[:, ji * T + t0: ji * T + t0 + 512]) for ji in range(8)]
                matmul_acc(p, pb[1][0:2, 0:512], pairs, pb[1],
                           reads=[(ws, ""), (h_f, "")])
                def fev2(E, t0=t0):
                    return E.activation(slog[0:2, t0:t0 + 512], pb[1][0:2, 0:512],
                                        AF.Identity, bias=small["b_s"][0:2, 0:1],
                                        scale=1.0)
                p.emit(ACT, fev2, reads=[(pb[1], ""), (small["b_s"], "")],
                       writes=[(slog, "t%d" % tt)])
            g15 = p.sbuf("g15", [128, 15], F32, stack=sub)
            s2t = p.sbuf("s2t", [128, 2], F32, stack=sub)
            gmx = p.sbuf("gmx", [128, 1], F32, stack=sub)
            gnx = p.sbuf("gnx", [128, 1], F32, stack=sub)
            gex = p.sbuf("gex", [128, 15], F32, stack=sub)
            gsm = p.sbuf("gsm", [128, 1], F32, stack=sub)
            grc = p.sbuf("grc", [128, 1], F32, stack=sub)
            tma = p.sbuf("tma", [128, 15], F32, stack=sub)
            m1b = p.sbuf("m1b", [128, 1], F32, stack=sub)
            m2b = p.sbuf("m2b", [128, 1], F32, stack=sub)
            m3b = p.sbuf("m3b", [128, 1], F32, stack=sub)
            sw0 = p.sbuf("sw0", [128, 1], F32, stack=sub)
            gam = p.sbuf("gam", [128, 1], F32, stack=sub)
            gam_bf = p.sbuf("gam_bf", [128, 1], BF16, stack=sub)
            gam_row = p.sbuf("gam_row", [1, T], BF16, stack=sub)
            for i in range(T // 128):
                def ft1(E, i=i):
                    return E.transpose(pb[2][0:128, 0:15],
                                       rlog[0:15, i * 128:(i + 1) * 128],
                                       c["ident_f32"][0:15, 0:15])
                p.emit(PE, ft1, reads=[(rlog, ""), (c["ident_f32"], "")],
                       writes=[(pb[2], "")])
                def fe1(E):
                    return E.tensor_copy(g15[0:128, 0:15], pb[2][0:128, 0:15])
                p.emit(DVE, fe1, reads=[(pb[2], "")], writes=[(g15, "")])
                def ft2(E, i=i):
                    return E.transpose(pb[2][0:128, 0:2],
                                       slog[0:2, i * 128:(i + 1) * 128],
                                       c["ident_f32"][0:2, 0:2])
                p.emit(PE, ft2, reads=[(slog, ""), (c["ident_f32"], "")],
                       writes=[(pb[2], "")])
                def fe2(E):
                    return E.tensor_copy(s2t[0:128, 0:2], pb[2][0:128, 0:2])
                p.emit(DVE, fe2, reads=[(pb[2], "")], writes=[(s2t, "")])
                def g1(E):
                    return E.tensor_reduce(gmx[0:128, 0:1], g15[0:128, 0:15],
                                           axis=AX.X, op=ALU.max)
                p.emit(DVE, g1, reads=[(g15, "")], writes=[(gmx, "")])
                def g2(E):
                    return E.tensor_scalar(gnx[0:128, 0:1], gmx[0:128, 0:1], -1.0,
                                           None, op0=ALU.mult)
                p.emit(DVE, g2, reads=[(gmx, "")], writes=[(gnx, "")])
                def g3(E):
                    return E.activation(gex[0:128, 0:15], g15[0:128, 0:15], AF.Exp,
                                        bias=gnx[0:128, 0:1], scale=1.0)
                p.emit(ACT, g3, reads=[(g15, ""), (gnx, "")], writes=[(gex, "")])
                def g4(E):
                    return E.tensor_reduce(gsm[0:128, 0:1], gex[0:128, 0:15],
                                           axis=AX.X, op=ALU.add)
                p.emit(DVE, g4, reads=[(gex, "")], writes=[(gsm, "")])
                def g5(E):
                    return E.reciprocal(grc[0:128, 0:1], gsm[0:128, 0:1])
                p.emit(DVE, g5, reads=[(gsm, "")], writes=[(grc, "")])
                def g6(E):
                    return E.tensor_scalar(g15[0:128, 0:15], gex[0:128, 0:15],
                                           grc[0:128, 0:1], None, op0=ALU.mult)
                p.emit(DVE, g6, reads=[(gex, ""), (grc, "")], writes=[(g15, "")])
                def t1(E):
                    return E.tensor_reduce(m1b[0:128, 0:1], g15[0:128, 0:15],
                                           axis=AX.X, op=ALU.max)
                p.emit(DVE, t1, reads=[(g15, "")], writes=[(m1b, "")])
                def t2(E):
                    return E.tensor_scalar(tma[0:128, 0:15], g15[0:128, 0:15],
                                           m1b[0:128, 0:1], None, op0=ALU.is_ge)
                p.emit(DVE, t2, reads=[(g15, ""), (m1b, "")], writes=[(tma, "")])
                def t3(E):
                    return E.scalar_tensor_tensor(tma[0:128, 0:15], tma[0:128, 0:15],
                                                  -2.0, g15[0:128, 0:15],
                                                  op0=ALU.mult, op1=ALU.add)
                p.emit(DVE, t3, reads=[(tma, ""), (g15, "")], writes=[(tma, "")])
                def t4(E):
                    return E.tensor_reduce(m2b[0:128, 0:1], tma[0:128, 0:15],
                                           axis=AX.X, op=ALU.max)
                p.emit(DVE, t4, reads=[(tma, "")], writes=[(m2b, "")])
                def t5(E):
                    return E.tensor_scalar(gex[0:128, 0:15], tma[0:128, 0:15],
                                           m2b[0:128, 0:1], None, op0=ALU.is_ge)
                p.emit(DVE, t5, reads=[(tma, ""), (m2b, "")], writes=[(gex, "")])
                def t6(E):
                    return E.scalar_tensor_tensor(tma[0:128, 0:15], gex[0:128, 0:15],
                                                  -2.0, tma[0:128, 0:15],
                                                  op0=ALU.mult, op1=ALU.add)
                p.emit(DVE, t6, reads=[(gex, ""), (tma, "")], writes=[(tma, "")])
                def t7(E):
                    return E.tensor_reduce(m3b[0:128, 0:1], tma[0:128, 0:15],
                                           axis=AX.X, op=ALU.max)
                p.emit(DVE, t7, reads=[(tma, "")], writes=[(m3b, "")])
                def t8(E):
                    return E.tensor_tensor(m1b[0:128, 0:1], m1b[0:128, 0:1],
                                           m2b[0:128, 0:1], op=ALU.add)
                p.emit(DVE, t8, reads=[(m1b, ""), (m2b, "")], writes=[(m1b, "")])
                def t9(E):
                    return E.tensor_tensor(m1b[0:128, 0:1], m1b[0:128, 0:1],
                                           m3b[0:128, 0:1], op=ALU.add)
                p.emit(DVE, t9, reads=[(m1b, ""), (m3b, "")], writes=[(m1b, "")])
                def w1(E):
                    return E.tensor_tensor(sw0[0:128, 0:1], s2t[0:128, 0:1],
                                           s2t[0:128, 1:2], op=ALU.subtract)
                p.emit(DVE, w1, reads=[(s2t, "")], writes=[(sw0, "")])
                def w2(E):
                    return E.activation(sw0[0:128, 0:1], sw0[0:128, 0:1], AF.Exp,
                                        bias=0.0, scale=-1.0)
                p.emit(ACT, w2, reads=[(sw0, "")], writes=[(sw0, "")])
                def w3(E):
                    return E.tensor_scalar(sw0[0:128, 0:1], sw0[0:128, 0:1], 1.0,
                                           None, op0=ALU.add)
                p.emit(DVE, w3, reads=[(sw0, "")], writes=[(sw0, "")])
                def w4(E):
                    return E.reciprocal(sw0[0:128, 0:1], sw0[0:128, 0:1])
                p.emit(DVE, w4, reads=[(sw0, "")], writes=[(sw0, "")])
                def w5(E):
                    return E.tensor_scalar(m2b[0:128, 0:1], sw0[0:128, 0:1], -1.0,
                                           1.0, op0=ALU.mult, op1=ALU.add)
                p.emit(DVE, w5, reads=[(sw0, "")], writes=[(m2b, "")])
                def w6(E):
                    return E.tensor_tensor(m2b[0:128, 0:1], m2b[0:128, 0:1],
                                           m1b[0:128, 0:1], op=ALU.mult)
                p.emit(DVE, w6, reads=[(m2b, ""), (m1b, "")], writes=[(m2b, "")])
                def w7(E):
                    return E.tensor_scalar(gam[0:128, 0:1], sw0[0:128, 0:1],
                                           2.0, None, op0=ALU.mult)
                p.emit(DVE, w7, reads=[(sw0, "")], writes=[(gam, "")])
                def w7b(E):
                    return E.scalar_tensor_tensor(gam[0:128, 0:1], m2b[0:128, 0:1],
                                                  6.0, gam[0:128, 0:1],
                                                  op0=ALU.mult, op1=ALU.add)
                p.emit(DVE, w7b, reads=[(m2b, ""), (gam, "")], writes=[(gam, "")])
                def w8(E):
                    return E.tensor_copy(gam_bf[0:128, 0:1], gam[0:128, 0:1])
                p.emit(DVE, w8, reads=[(gam, "")], writes=[(gam_bf, "")])
                def w9(E, i=i):
                    return E.transpose(pbt[0:1, 0:128], gam_bf[0:128, 0:1],
                                       c["ident"][:, :])
                p.emit(PE, w9, reads=[(gam_bf, ""), (c["ident"], "")], writes=[(pbt, "")])
                def w10(E, i=i):
                    return E.tensor_copy(gam_row[0:1, i * 128:(i + 1) * 128],
                                         pbt[0:1, 0:128])
                p.emit(DVE, w10, reads=[(pbt, "")], writes=[(gam_row, "i%d" % i)])
            gbc = p.sbuf("gbc", [128, 512], BF16, stack=sub)
            for tt in range(4):
                t0 = tt * 512
                matmul_acc(p, pb[2][0:128, 0:512],
                           [(c["ones_bf_1x128"][0:1, :], gam_row[0:1, t0:t0 + 512])],
                           pb[2], reads=[(gam_row, ""), (c["ones_bf_1x128"], "")])
                def fg(E):
                    return E.tensor_copy(gbc[:, 0:512], pb[2][0:128, 0:512])
                p.emit(DVE, fg, reads=[(pb[2], "")], writes=[(gbc, "")])
                for ji in range(8):
                    def fh(E, ji=ji, t0=t0):
                        a = h_f[:, ji * T + t0: ji * T + t0 + 512]
                        return E.tensor_tensor(a, a, gbc[:, 0:512], op=ALU.mult)
                    p.emit(DVE, fh, reads=[(h_f, ""), (gbc, "")],
                           writes=[(h_f, "c%d" % ji)])
            wp = p.sbuf("wp", [128, 8 * 1024], BF16, stack=sub)
            srcp = di["w_pT"].rearrange("(i p) o -> p i o", p=128)
            p.dma(QPOOL, lambda E: E.dma_start(
                out=wp.t[:, :].rearrange("p (i o) -> p i o", o=1024), in_=srcp),
                writes=[(wp, "")])
            for tt in range(4):
                t0 = tt * 512
                for jo2 in range(8):
                    pairs = [(wp[:, ji * 1024 + jo2 * 128: ji * 1024 + jo2 * 128 + 128],
                              h_f[:, ji * T + t0: ji * T + t0 + 512]) for ji in range(8)]
                    matmul_acc(p, pb[0][0:128, 0:512], pairs, pb[0],
                               reads=[(wp, ""), (h_f, "")])
                    def fadd(E, jo2=jo2, t0=t0):
                        a = x1[:, jo2 * T + t0: jo2 * T + t0 + 512]
                        return E.tensor_tensor(a, a, pb[0][0:128, 0:512], op=ALU.add)
                    p.emit(DVE, fadd, reads=[(pb[0], ""), (x1, "")],
                           writes=[(x1, "c%d" % jo2)])
            for jo2 in range(8):
                for tt in range(4):
                    t0 = tt * 512
                    def fb(E, jo2=jo2, t0=t0):
                        a = x1[:, jo2 * T + t0: jo2 * T + t0 + 512]
                        return E.tensor_scalar(a, a, small["b_p"][:, jo2:jo2 + 1],
                                               None, op0=ALU.add)
                    p.emit(DVE, fb, reads=[(x1, ""), (small["b_p"], "")],
                           writes=[(x1, "c%d" % jo2)])
        p.dma(QSP, lambda E: E.dma_start(out=xout_d[:, :], in_=x1[:, :]),
              reads=[(x1, "")])
        p.wait_all_dma()
        with nc.Block() as block:
            p.finalize(block)
    return nc



_NC_CACHE = {}


def _get_nc(name, builder):
    if name not in _NC_CACHE:
        _NC_CACHE[name] = builder()
    return _NC_CACHE[name]


def kernel(**inputs):
    inp = {k: np.asarray(v) for k, v in inputs.items()}
    x = np.ascontiguousarray(inp["L0_attn_x"].astype(np.float32).reshape(16384, 1024))
    hc = host_consts()
    masks_np = _win_masks()

    def feat_b(a):
        return to_feat(np.ascontiguousarray(a)).astype(BF) if a.dtype != BF else to_feat(a)

    nc1 = _get_nc("l1", build_launch1)
    maps1 = []
    for cc in range(N_CORES):
        xo = to_feat(x[cc * T_CORE:(cc + 1) * T_CORE])
        maps1.append(dict(x_in=xo, wg_in=inp["L0_ffn_wg"], **hc))
    res1 = run_bass_kernel_spmd(nc1, maps1, core_ids=list(range(N_CORES)))
    counts = [res1.results[cc]["cnt_out"] for cc in range(N_CORES)]
    cnt_all = np.concatenate(counts, axis=1).astype(np.float32)

    nc2 = _get_nc("l2", build_launch2)
    maps2 = []
    for cc in range(N_CORES):
        ext = np.zeros((T_CORE + HALO, 1024), dtype=np.float32)
        lo = cc * T_CORE - HALO
        if cc % 4 == 0:
            ext[HALO:] = x[cc * T_CORE:(cc + 1) * T_CORE]
        else:
            ext[:] = x[lo: (cc + 1) * T_CORE]
        vcol = np.zeros((128, 4), np.float32) if cc % 4 == 0 else np.ones((128, 4), np.float32)
        base_mask = np.zeros((8, 8), np.float32)
        base_mask[:, :cc] = 1.0
        maps2.append(dict(
            x_in=to_feat(ext), wg_in=inp["L0_ffn_wg"],
            w_q0=inp["L0_attn_Wq"], w_k0=inp["L0_attn_Wk"],
            w_v0=inp["L0_attn_Wv"], w_o0=inp["L0_attn_Wo"],
            w_moe=np.ascontiguousarray(inp["L0_ffn_W"].transpose(0, 2, 1)),
            b_moe=inp["L0_ffn_b"],
            b_q0=to_feat(inp["L0_attn_bq"][None, :]),
            b_kv0=np.concatenate([to_feat(inp["L0_attn_bk"][None, :]),
                                  to_feat(inp["L0_attn_bv"][None, :])], axis=1),
            b_o0=to_feat(inp["L0_attn_bo"][None, :]),
            masks_in=masks_np, vcol_in=vcol, base_mask=base_mask, cnt_all=cnt_all,
            w_qa=inp["L1_attn_Wq"], w_ka=inp["L1_attn_Wk"], w_va=inp["L1_attn_Wv"],
            b_qa=to_feat(inp["L1_attn_bq"][None, :]),
            b_ka=to_feat(inp["L1_attn_bk"][None, :]),
            b_va=to_feat(inp["L1_attn_bv"][None, :]),
            w_qfT=np.ascontiguousarray(inp["L1_ffn_Wq"].T),
            w_kfT=np.ascontiguousarray(inp["L1_ffn_Wk"].T),
            w_vfT=np.ascontiguousarray(inp["L1_ffn_Wv"].T),
            b_qf=to_feat(inp["L1_ffn_bq"][None, :]),
            b_kf=to_feat(inp["L1_ffn_bk"][None, :]),
            b_vf=to_feat(inp["L1_ffn_bv"][None, :]),
            temp_in=np.ascontiguousarray(inp["L1_ffn_temperature"].reshape(1, 16)),
            qe_in=np.ascontiguousarray(inp["L1_ffn_query_embedding"].reshape(16, 64).T),
            **hc))
    res2 = run_bass_kernel_spmd(nc2, maps2, core_ids=list(range(N_CORES)))
    r2 = res2.results

    x1 = [r2[cc]["x1_out"] for cc in range(N_CORES)]
    qa = [r2[cc]["qa_out"] for cc in range(N_CORES)]
    ka_tok = [from_feat(r2[cc]["ka_out"], T_CORE, 1024) for cc in range(N_CORES)]
    va_tok = [from_feat(r2[cc]["va_out"], T_CORE, 1024) for cc in range(N_CORES)]
    qf = [r2[cc]["qf_out"] for cc in range(N_CORES)]
    kf_tok = [from_feat(r2[cc]["kf_out"], T_CORE, 1024) for cc in range(N_CORES)]
    vf_tok = [from_feat(r2[cc]["vf_out"], T_CORE, 1024) for cc in range(N_CORES)]
    kd_b, vd_b, kf_b, vf_b = [], [], [], []
    for b in range(2):
        ka_all = np.concatenate(ka_tok[4 * b:4 * b + 4], axis=0)
        va_all = np.concatenate(va_tok[4 * b:4 * b + 4], axis=0)
        kd_b.append(ka_all[::4])
        vd_b.append(va_all[::4])
        kf_b.append(to_feat(np.concatenate(kf_tok[4 * b:4 * b + 4], axis=0)))
        vf_b.append(np.ascontiguousarray(np.concatenate(vf_tok[4 * b:4 * b + 4], axis=0)))

    nc3 = _get_nc("l3", build_launch3)
    maps3 = []
    for cc in range(N_CORES):
        b, q = cc // 4, cc % 4
        kd_nb = np.zeros((1024, 1024), dtype=BF)
        vd_nb = np.zeros((1024, 1024), dtype=BF)
        for sl in range(8):
            sg = q * 4 - 2 + sl
            if 0 <= sg < 16:
                kd_nb[sl * 128:(sl + 1) * 128] = kd_b[b][sg * 128:(sg + 1) * 128]
                vd_nb[sl * 128:(sl + 1) * 128] = vd_b[b][sg * 128:(sg + 1) * 128]
        wvalid = np.zeros((128, 16), np.float32)
        for seg in range(4):
            for oi, off in enumerate((-2, -1, 1, 2)):
                n = q * 4 + seg + off
                if 0 <= n < 16:
                    wvalid[:, seg * 4 + oi] = 0.25
        maps3.append(dict(
            x1_in=x1[cc], qa_in=qa[cc],
            ka_in=r2[cc]["ka_out"],
            va_tok=np.ascontiguousarray(va_tok[cc]),
            kd_nb=to_feat(kd_nb), vd_tok=vd_nb,
            qf_in=qf[cc], kf_full=kf_b[b], vf_tok=vf_b[b],
            w_oa=inp["L1_attn_Wo"],
            b_oa=to_feat(inp["L1_attn_bo"][None, :]),
            wvalid=wvalid,
            w_rT=np.ascontiguousarray(inp["L1_ffn_Wr"].T),
            b_r=np.ascontiguousarray(inp["L1_ffn_br"][:, None]),
            w_sT=np.ascontiguousarray(inp["L1_ffn_Ws"].T),
            b_s=np.ascontiguousarray(inp["L1_ffn_bs"][:, None]),
            w_pT=np.ascontiguousarray(inp["L1_ffn_Wp"].T),
            b_p=to_feat(inp["L1_ffn_bp"][None, :]),
            **hc))
    res3 = run_bass_kernel_spmd(nc3, maps3, core_ids=list(range(N_CORES)))
    out = np.concatenate(
        [from_feat(res3.results[cc]["x_out"], T_CORE, 1024) for cc in range(N_CORES)],
        axis=0)
    return out.reshape(2, 8192, 1024).astype(np.float32)
